# version 18
# baseline (speedup 1.0000x reference)
import contextlib
import numpy as np
import ml_dtypes
import concourse.bass as bass
import concourse.mybir as mybir
from concourse.bass_utils import run_bass_kernel_spmd

F32 = mybir.dt.float32
BF16 = mybir.dt.bfloat16
FP16 = mybir.dt.float16
FP8 = mybir.dt.float8e4
AF = mybir.ActivationFunctionType
ALU = mybir.AluOpType
AX = mybir.AxisListType
bfdt = ml_dtypes.bfloat16

B, L, D = 2, 4096, 2048
HQ, HKV, HD = 32, 8, 64
H = 8192
NT = L // 128
NGQ = L // 512
QTN = L // 512
OWN = 1024
SCALE = 0.125
MAGIC = 1536.0
EPS_LN = 1e-5
EPS_RMS = 1e-8

ENGINES = ("sync", "tensor", "vector", "scalar", "gpsimd")


class KB:

    def __init__(self, nc, es):
        self.nc = nc
        self.streams = {e: [] for e in ENGINES}
        self.sems = {}
        self.cnt = {}
        self.es = es

    def sem(self, name):
        self.sems[name] = self.es.enter_context(self.nc.semaphore(name))
        self.cnt[name] = 0
        return name

    def val(self, s):
        return self.cnt[s]

    def op(self, eng, meth, *args, inc=None, inck=1, **kw):
        h = self.sems[inc] if inc else None
        if inc:
            self.cnt[inc] += inck
        post = self.cnt[inc] if inc else None

        def emit(e, meth=meth, args=args, kw=kw, h=h, inck=inck):
            r = getattr(e, meth)(*args, **kw)
            if h is not None:
                r.then_inc(h, inck)
        self.streams[eng].append(emit)
        return post

    def wait(self, eng, s, v=None):
        v = self.cnt[s] if v is None else v
        h = self.sems[s]

        def emit(e, h=h, v=v):
            e.wait_ge(h, v)
        self.streams[eng].append(emit)

    def dma(self, eng, out, in_, inc, **kw):
        return self.op(eng, "dma_start", out=out, in_=in_, inc=inc, inck=16, **kw)

    def sop(self, eng, meth, *args, **kw):
        s = {"vector": "ve", "scalar": "se", "gpsimd": "ge"}[eng]
        self.wait(eng, s)
        return self.op(eng, meth, *args, inc=s, **kw)


def build_nc(debug=False, detect_races=False):
    nc = bass.Bass("TRN2", target_bir_lowering=False,
                   detect_race_conditions=detect_races)

    xfT = nc.dram_tensor("xfT", [32, 128, 2048], F32, kind="ExternalInput").ap()
    xo = nc.dram_tensor("xo", [OWN, D], F32, kind="ExternalInput").ap()
    wqT = nc.dram_tensor("wqT", [4, 128, 2048], F32, kind="ExternalInput").ap()
    wkT = nc.dram_tensor("wkT", [1, 128, 2048], F32, kind="ExternalInput").ap()
    wvT = nc.dram_tensor("wvT", [1, 128, 2048], F32, kind="ExternalInput").ap()
    woT = nc.dram_tensor("woT", [512, D], F32, kind="ExternalInput").ap()
    wgT = nc.dram_tensor("wgT", [64, 128, 2048], F32, kind="ExternalInput").ap()
    wuT = nc.dram_tensor("wuT", [64, 128, 2048], F32, kind="ExternalInput").ap()
    wdT = nc.dram_tensor("wdT", [H, D], F32, kind="ExternalInput").ap()
    cs_tab = nc.dram_tensor("cs_tab", [2, 128, L], F32, kind="ExternalInput").ap()
    tri_in = nc.dram_tensor("tri_in", [128, 128], BF16, kind="ExternalInput").ap()
    idn_in = nc.dram_tensor("idn_in", [128, 128], BF16, kind="ExternalInput").ap()
    idf_in = nc.dram_tensor("idf_in", [128, 128], F32, kind="ExternalInput").ap()
    onc_in = nc.dram_tensor("onc_in", [128, 1], BF16, kind="ExternalInput").ap()
    onr_in = nc.dram_tensor("onr_in", [1, 128], BF16, kind="ExternalInput").ap()
    y = nc.dram_tensor("y", [OWN, D], F32, kind="ExternalOutput").ap()
    cc_in = nc.dram_tensor("cc_in", [L, D], BF16)
    cc_out = nc.dram_tensor("cc_out", [OWN, D], BF16)
    dbg = {}
    if debug:
        def dout(name, shape, dt=F32):
            dbg[name] = nc.dram_tensor("dbg_" + name, shape, dt,
                                       kind="ExternalOutput").ap()
        dout("Q", [128, 4, L])
        dout("Kpa", [128, L])
        dout("Kpb", [128, L])
        dout("Va", [128, NT, 65])
        dout("A", [128, 4, L])
        dout("sigm", [64, 512])
        dout("biasb", [128, 4])
        dout("xqT", [128, 16, OWN])
        dout("qsum", [1, 512])
        dout("mrow", [1, 2, 512])
        dout("shi", [1, 2, 512])
        dout("statc", [128, 32])

    with contextlib.ExitStack() as es:
        kb = KB(nc, es)
        te = kb.sem("te")
        ve = kb.sem("ve")
        se = kb.sem("se")
        ge = kb.sem("ge")
        scc = kb.sem("scc")
        sc0 = kb.sem("sc0")
        sxs = [kb.sem(f"sxs{i}") for i in range(3)]
        stb = [kb.sem(f"stb{i}") for i in range(2)]
        swt = [kb.sem(f"swt{i}") for i in range(2)]
        ssst = [kb.sem(f"sss{i}") for i in range(8)]
        srb = [kb.sem(f"srb{i}") for i in range(4)]
        scm = [kb.sem(f"scm{i}") for i in range(2)]
        sac = kb.sem("sac")
        sxo = [kb.sem(f"sxo{i}") for i in range(2)]
        swg = [kb.sem(f"swg{i}") for i in range(2)]
        swu = [kb.sem(f"swu{i}") for i in range(2)]
        swd = [kb.sem(f"swd{i}") for i in range(2)]
        syy = kb.sem("syy")
        sdb = kb.sem("sdb")

        sb = nc.sbuf_tensor
        ps = nc.psum_tensor
        tri = es.enter_context(sb("tri", [128, 128], BF16))
        idn = es.enter_context(sb("idn", [128, 128], BF16))
        idnf = es.enter_context(sb("idnf", [128, 128], F32))
        onescol = es.enter_context(sb("onescol", [128, 1], BF16))
        onesrow = es.enter_context(sb("onesrow", [1, 128], BF16))
        biasb = es.enter_context(sb("biasb", [128, 4], F32))
        q2run = es.enter_context(sb("q2run", [1, 4], F32))
        k2run = es.enter_context(sb("k2run", [1, 2], F32))
        smt = es.enter_context(sb("smt", [128, 16], F32))
        s1t = es.enter_context(sb("s1t", [1, 16], F32))
        nbb = es.enter_context(sb("nbb", [1, 4], BF16))
        statc = es.enter_context(sb("statc", [128, 32], F32))
        xqT = es.enter_context(sb("xqT", [128, 16, OWN], BF16))
        es_attn = es.enter_context(contextlib.ExitStack())
        Q = es_attn.enter_context(sb("Q", [128, 4, L], BF16))
        Kpa = es_attn.enter_context(sb("Kpa", [128, L], BF16))
        Kpb = es_attn.enter_context(sb("Kpb", [128, L], BF16))
        Va = es_attn.enter_context(sb("Va", [128, NT, 65], BF16))
        Vb = es_attn.enter_context(sb("Vb", [128, NT, 65], BF16))

        kb.dma("sync", tri[:, :], tri_in[:, :], sc0)
        kb.dma("sync", idn[:, :], idn_in[:, :], sc0)
        kb.dma("sync", idnf[:, :], idf_in[:, :], sc0)
        kb.dma("sync", onescol[:, :], onc_in[:, :], sc0)
        kb.dma("sync", onesrow[:, :], onr_in[:, :], sc0)
        kb.wait("vector", sc0)
        kb.op("vector", "memset", q2run[:, :], 0.0)
        kb.op("vector", "memset", k2run[:, :], 0.0, inc=ve)

        with contextlib.ExitStack() as p1:
            wq_bf = p1.enter_context(sb("wq_bf", [128, 16, 512], BF16))
            wk_bf = p1.enter_context(sb("wk_bf", [128, 16, 128], BF16))
            wv_bf = p1.enter_context(sb("wv_bf", [128, 16, 128], BF16))
            qsum = p1.enter_context(sb("qsum", [1, 512], BF16))
            ksum = p1.enter_context(sb("ksum", [1, 128], BF16))
            vsum = p1.enter_context(sb("vsum", [1, 128], BF16))
            mrow = p1.enter_context(sb("mrow", [1, 2, 512], BF16))
            shi = p1.enter_context(sb("shi", [1, 2, 512], BF16))
            slo = p1.enter_context(sb("slo", [1, 2, 512], BF16))
            rowscr = p1.enter_context(sb("rowscr", [128, 512], F32))
            m2scr = p1.enter_context(sb("m2scr", [1, 512], F32))
            tsscr = p1.enter_context(sb("tsscr", [1, 512], F32))
            m2s = m2scr[0:1, :]
            vrs = rowscr[0:1, :]
            rcs = rowscr[32:33, :]
            tsf = tsscr[0:1, :]
            xst = p1.enter_context(sb("xst", [128, 3, 2048], F32))
            xb16 = p1.enter_context(sb("xb16", [128, 2, 16, 512], BF16))
            sqb = p1.enter_context(sb("sqb", [128, 2, 4, 512], BF16))
            tabb = p1.enter_context(sb("tabb", [128, 1, 2, 512], F32))
            tab_s = p1.enter_context(sb("tab_s", [128, 2, 2, 512], BF16))
            Qc = p1.enter_context(sb("Qc", [128, 2, 512], BF16))
            rtm = p1.enter_context(sb("rtm", [128, 2, 512], BF16))
            rt1 = p1.enter_context(sb("rt1", [128, 2, 512], BF16))
            sqs = p1.enter_context(sb("sqs", [128, 2, 512], BF16))

            ps_q = [p1.enter_context(ps(f"ps_q{i}", [128, 512], F32)) for i in range(4)]
            ps_k = p1.enter_context(ps("ps_k", [128, 512], F32))
            ps_v = p1.enter_context(ps("ps_v", [128, 4, 128], F32))
            ps_misc = p1.enter_context(ps("ps_misc", [128, 512], F32))
            ps_bc = p1.enter_context(ps("ps_bc", [128, 512], F32))

            stage_free = {}
            ci = 0
            for wt, nchunk, dst in ((wqT, 4, wq_bf), (wkT, 1, wk_bf), (wvT, 1, wv_bf)):
                for ch in range(nchunk):
                    slot = ci % 2
                    if ci >= 2:
                        kb.wait("sync", se, stage_free[ci - 2])
                    kb.dma("sync", xst[:, slot, :], wt[ch, :, :], swt[slot])
                    kb.wait("scalar", swt[slot])
                    kb.op("scalar", "copy", dst[:, :, 128 * ch:128 * (ch + 1)],
                          xst[:, slot, :].rearrange("p (j c) -> p j c", j=16),
                          inc=se)
                    stage_free[ci] = kb.val(se)
                    ci += 1

            kb.op("vector", "memset", Kpa[64:128, :], 0.0)
            kb.op("vector", "memset", Kpb[0:64, :], 0.0)
            kb.op("vector", "memset", Va[:, :, 64:65], 1.0)
            kb.op("vector", "memset", Vb[:, :, 64:65], 1.0, inc=ve)
            v_init = kb.val(ve)
            kb.wait("tensor", ve, v_init)
            kb.wait("tensor", se)

            for r, (wt_bf, ncol) in enumerate(((wq_bf, 512), (wk_bf, 128),
                                               (wv_bf, 128))):
                for j in range(16):
                    kb.op("tensor", "matmul",
                          ps_misc[32 * r:32 * r + 1, 0:ncol],
                          onescol[:, :], wt_bf[:, j, :],
                          start=(j == 0), stop=(j == 15),
                          inc=(te if j == 15 else None))
            v_wsum = kb.val(te)
            kb.wait("vector", te, v_wsum)
            kb.op("vector", "tensor_copy", qsum[:, :], ps_misc[0:1, 0:512])
            kb.op("vector", "tensor_copy", ksum[:, :], ps_misc[32:33, 0:128])
            kb.op("vector", "tensor_copy", vsum[:, :], ps_misc[64:65, 0:128],
                  inc=ve)
            v_wsum_ev = kb.val(ve)
            kb.wait("tensor", ve, v_wsum_ev)

            cast_done = {}
            sq_done = {}
            ssq_done = {}
            xb_free = {}
            chain_read = {}
            evac_done = {}
            corr_done = {}
            trans_done = {}
            stats_done = {}
            vdone = {}
            ropev = {}
            sqrt_done = {}
            tabfree = {}
            statq = {}
            sqs_free = {}
            boundv = {}
            PROJ = ["q0", "q1", "q2", "q3", "k"]

            brow_free = {0: v_wsum_ev, 64: v_wsum_ev}

            def emit_bound(gb):
                pos0b = 512 * gb
                if gb == 0:
                    kb.wait("tensor", ve, v_wsum_ev)
                jobs = [("Q", pc, q2run, pc) for pc in range(4)]
                jobs += [("K", 0, k2run, 0), ("K", 1, k2run, 1)]
                for idx, (kind, pc, run, runslot) in enumerate(jobs):
                    tile = Q[:, pc, pos0b:pos0b + 512] if kind == "Q" else \
                        (Kpa if pc == 0 else Kpb)[:, pos0b:pos0b + 512]
                    slot = idx % 2
                    row = 64 if idx % 2 == 0 else 0
                    kb.wait("scalar", ve, ropev[gb])
                    if slot in sqs_free:
                        kb.wait("scalar", te, sqs_free[slot])
                    kb.op("scalar", "activation", sqs[:, slot, :], tile,
                          AF.Square, bias=0.0, scale=1.0, inc=se)
                    v_sq = kb.val(se)
                    kb.wait("tensor", se, v_sq)
                    if row == 0 and idx == 1:
                        kb.wait("tensor", ve, chain_read[gb + 1]
                                if gb + 1 in chain_read else chain_read[gb])
                    kb.wait("tensor", ve, brow_free[row])
                    kb.op("tensor", "matmul", ps_misc[row:row + 1, :],
                          onescol[:, :], sqs[:, slot, :], start=True,
                          stop=True, inc=te)
                    sqs_free[slot] = kb.val(te)
                    kb.wait("vector", te)
                    kb.sop("vector", "tensor_reduce", s1t[:, 0:1],
                           ps_misc[row:row + 1, :], AX.X, ALU.max)
                    kb.sop("vector", "tensor_tensor",
                           run[:, runslot:runslot + 1],
                           run[:, runslot:runslot + 1], s1t[:, 0:1], ALU.max)
                    brow_free[row] = kb.val(ve)
                boundv[gb] = kb.val(ve)

            def emit_xdma(cig):
                slot = cig % 3
                s = cig % 4
                gd = cig // 4
                q = "sync" if cig % 2 == 0 else "gpsimd"
                if cig <= 1:
                    kb.wait(q, se, stage_free[5])
                if cig >= 3:
                    kb.wait(q, se, cast_done[cig - 3])
                kb.dma(q, xst[:, slot, :], xfT[cig, :, :], sxs[slot])

            for c0 in range(3):
                emit_xdma(c0)

            for g in range(NGQ):
                par = g % 2
                pos0 = 512 * g
                if g >= 1:
                    kb.wait("sync", ve, tabfree[1 - par])
                kb.dma("sync", tabb[:, 0, 0, :],
                       cs_tab[0, :, pos0:pos0 + 512], stb[par])
                kb.dma("sync", tabb[:, 0, 1, :],
                       cs_tab[1, :, pos0:pos0 + 512], stb[par])
                v_tab = kb.val(stb[par])

                if g >= 1:
                    kb.wait("tensor", ve, chain_read[g - 1])
                if g >= 2:
                    kb.wait("tensor", ve, boundv[g - 2])
                for s in range(4):
                    cig = 4 * g + s
                    slot = cig % 3
                    kb.wait("scalar", sxs[slot], kb.val(sxs[slot]))
                    if s == 0 and par in xb_free:
                        kb.wait("scalar", te, xb_free[par])
                        if 4 * (g - 2) + 3 in sq_done:
                            kb.wait("scalar", ve, sq_done[4 * (g - 2) + 3])
                    kb.op("scalar", "copy", xb16[:, par, 4 * s:4 * (s + 1), :],
                          xst[:, slot, :].rearrange("p (j c) -> p j c", j=4),
                          inc=se)
                    cast_done[cig] = kb.val(se)
                    if cig + 3 < 4 * NGQ:
                        emit_xdma(cig + 3)
                    kb.wait("vector", se, cast_done[cig])
                    if cig >= 2:
                        kb.wait("vector", te, ssq_done[cig - 2])
                    kb.op("vector", "tensor_tensor", sqb[:, s % 2, :, :],
                          xb16[:, par, 4 * s:4 * (s + 1), :],
                          xb16[:, par, 4 * s:4 * (s + 1), :], ALU.mult, inc=ve)
                    sq_done[cig] = kb.val(ve)
                    kb.wait("tensor", se, cast_done[cig])
                    for jj in range(4):
                        j = 4 * s + jj
                        kb.op("tensor", "matmul", ps_misc[0:1, :],
                              onescol[:, :], xb16[:, par, j, :],
                              start=(j == 0), stop=(j == 15))
                    for jj in range(4):
                        j = 4 * s + jj
                        if j == 0:
                            for pcp in range(4):
                                if (g - 1, f"q{pcp}") in evac_done:
                                    kb.wait("tensor", ve,
                                            evac_done[(g - 1, f"q{pcp}")])
                            if (g - 1, "k") in evac_done:
                                kb.wait("tensor", ve, evac_done[(g - 1, "k")])
                        for pcp in range(4):
                            kb.op("tensor", "matmul", ps_q[pcp][:, :],
                                  wq_bf[:, j, 128 * pcp:128 * (pcp + 1)],
                                  xb16[:, par, j, :], start=(j == 0),
                                  stop=False)
                        kb.op("tensor", "matmul", ps_k[:, :], wk_bf[:, j, :],
                              xb16[:, par, j, :], start=(j == 0), stop=False)
                    kb.wait("tensor", ve, sq_done[cig])
                    for jj in range(4):
                        j = 4 * s + jj
                        kb.op("tensor", "matmul", ps_misc[32:33, :],
                              onescol[:, :], sqb[:, s % 2, jj, :],
                              start=(j == 0), stop=(j == 15),
                              inc=(te if jj == 3 else None))
                    ssq_done[cig] = kb.val(te)
                stats_done[g] = kb.val(te)

                kb.wait("vector", te, stats_done[g])
                if g >= 2:
                    kb.wait("vector", te, vdone[g - 2])
                    kb.wait("vector", te, trans_done[g - 2])
                    kb.wait("vector", se, sqrt_done[g - 1])
                kb.sop("vector", "tensor_scalar", mrow[:, par, :],
                       ps_misc[0:1, :], -1.0 / D, None, ALU.mult)
                kb.sop("vector", "tensor_tensor", m2s, mrow[:, par, :],
                       mrow[:, par, :], ALU.mult)
                kb.sop("vector", "tensor_scalar", vrs,
                       ps_misc[32:33, :], 1.0 / D, EPS_LN, ALU.mult,
                       op1=ALU.add)
                kb.sop("vector", "tensor_tensor", vrs, vrs,
                       m2s, ALU.subtract)
                kb.sop("vector", "reciprocal", rcs, vrs)
                chain_read[g] = kb.val(ve)
                kb.wait("scalar", ve, chain_read[g])
                kb.op("scalar", "activation", tsf, rcs, AF.Sqrt,
                      bias=0.0, scale=1.0, inc=se)
                sqrt_done[g] = kb.val(se)
                kb.wait("vector", se, sqrt_done[g])
                kb.sop("vector", "tensor_copy", shi[:, par, :], tsf)
                kb.sop("vector", "tensor_tensor", slo[:, par, :], tsf,
                       shi[:, par, :], ALU.subtract)
                v_rows = kb.val(ve)

                if g >= 1:
                    kb.wait("tensor", ve, evac_done[(g - 1, "v")])
                for j in range(16):
                    for c in range(4):
                        kb.op("tensor", "matmul", ps_v[:, c, :],
                              xb16[:, par, j, 128 * c:128 * (c + 1)],
                              wv_bf[:, j, :], start=(j == 0), stop=False)
                xb_free[par] = kb.val(te) + 1

                if g >= 1:
                    emit_bound(g - 1)

                kb.wait("tensor", ve, v_rows)
                for pcp in range(4):
                    kb.op("tensor", "matmul", ps_q[pcp][:, :],
                          qsum[:, 128 * pcp:128 * (pcp + 1)], mrow[:, par, :],
                          start=False, stop=True,
                          inc=(te if pcp == 3 else None))
                kb.op("tensor", "matmul", ps_k[:, :], ksum[:, :],
                      mrow[:, par, :], start=False, stop=True)
                for c in range(4):
                    kb.op("tensor", "matmul", ps_v[:, c, :],
                          mrow[:, par, 128 * c:128 * (c + 1)], vsum[:, :],
                          start=False, stop=True)
                if "bc" in statq:
                    kb.wait("tensor", ve, statq["bc"])
                kb.op("tensor", "matmul", ps_bc[:, :], onesrow[:, :],
                      shi[:, par, :], start=True, stop=False)
                kb.op("tensor", "matmul", ps_bc[:, :], onesrow[:, :],
                      slo[:, par, :], start=False, stop=True, inc=te)
                v_corr = kb.val(te)
                corr_done[g] = v_corr
                vdone[g] = v_corr

                kb.wait("vector", stb[par], v_tab)
                kb.wait("vector", te, v_corr)
                kb.sop("vector", "tensor_tensor", tab_s[:, par, 0, :],
                       tabb[:, 0, 0, :], ps_bc[:, :], ALU.mult)
                kb.sop("vector", "tensor_tensor", tab_s[:, par, 1, :],
                       tabb[:, 0, 1, :], ps_bc[:, :], ALU.mult)
                tabfree[par] = kb.val(ve)

                for ip, proj in enumerate(PROJ):
                    pq = ps_q[ip] if proj != "k" else ps_k
                    sl = ip % 2
                    kb.sop("vector", "tensor_copy", Qc[:, sl, :], pq[:, :])
                    kb.wait("vector", ve)
                    for hh in range(2):
                        b0 = 64 * hh
                        kb.op("vector", "tensor_copy", rtm[b0:b0 + 32, sl, :],
                              Qc[b0 + 32:b0 + 64, sl, :])
                        kb.op("vector", "tensor_copy",
                              rtm[b0 + 32:b0 + 64, sl, :],
                              Qc[b0:b0 + 32, sl, :],
                              inc=(ve if hh == 1 else None))
                    evac_done[(g, proj)] = kb.val(ve)
                    kb.sop("vector", "tensor_tensor", rt1[:, sl, :],
                           Qc[:, sl, :], tab_s[:, par, 0, :], ALU.mult)
                    kb.sop("vector", "tensor_tensor", Qc[:, sl, :],
                           rtm[:, sl, :], tab_s[:, par, 1, :], ALU.mult)
                    kb.wait("vector", ve)
                    if proj == "k":
                        kb.op("vector", "tensor_tensor",
                              Kpa[0:64, pos0:pos0 + 512], rt1[0:64, sl, :],
                              Qc[0:64, sl, :], ALU.add)
                        kb.op("vector", "tensor_tensor",
                              Kpb[64:128, pos0:pos0 + 512],
                              rt1[64:128, sl, :], Qc[64:128, sl, :],
                              ALU.add, inc=ve)
                    else:
                        kb.op("vector", "tensor_tensor",
                              Q[:, ip, pos0:pos0 + 512], rt1[:, sl, :],
                              Qc[:, sl, :], ALU.add, inc=ve)
                ropev[g] = kb.val(ve)

                kb.wait("tensor", ve, tabfree[par])
                for c in range(4):
                    kb.op("tensor", "matmul", ps_bc[:, c:c + 1],
                          shi[:, par, 128 * c:128 * (c + 1)],
                          onescol[0:1, 0:1], start=True, stop=True,
                          inc=(te if c == 3 else None))
                trans_done[g] = kb.val(te)
                kb.wait("vector", te, trans_done[g])
                kb.op("vector", "tensor_copy", statc[:, 4 * g:4 * g + 4],
                      ps_bc[:, 0:4], inc=ve)
                statq["bc"] = kb.val(ve)
                kb.wait("vector", te, vdone[g])
                kb.wait("vector", ve)
                for c in range(4):
                    ci_ = 4 * g + c
                    kb.op("vector", "tensor_scalar", Va[:, ci_, 0:64],
                          ps_v[:, c, 0:64], statc[:, ci_:ci_ + 1], None,
                          ALU.mult)
                    kb.op("vector", "tensor_scalar", Vb[:, ci_, 0:64],
                          ps_v[:, c, 64:128], statc[:, ci_:ci_ + 1], None,
                          ALU.mult, inc=(ve if c == 3 else None))
                evac_done[(g, "v")] = kb.val(ve)

            emit_bound(NGQ - 1)

            kb.sop("vector", "tensor_tensor", k2run[:, 0:1], k2run[:, 0:1],
                   k2run[:, 1:2], ALU.max)
            kb.sop("vector", "tensor_scalar", s1t[:, 0:4], q2run[:, :],
                   k2run[:, 0:1], None, ALU.mult)
            kb.wait("scalar", ve)
            kb.op("scalar", "activation", s1t[:, 4:8], s1t[:, 0:4], AF.Sqrt,
                  bias=0.0, scale=1.0, inc=se)
            kb.wait("vector", se)
            kb.sop("vector", "tensor_scalar", s1t[:, 8:12], s1t[:, 4:8],
                   -SCALE, None, ALU.mult)
            kb.sop("vector", "tensor_copy", nbb[:, :], s1t[:, 8:12])
            kb.wait("tensor", ve)
            kb.op("tensor", "matmul", ps_bc[:, 4:8], onesrow[:, :], nbb[:, :],
                  start=True, stop=True, inc=te)
            kb.wait("vector", te)
            kb.op("vector", "tensor_copy", biasb[:, :], ps_bc[:, 4:8], inc=ve)
            v_bias = kb.val(ve)
            v_projdone = kb.val(te)
            v_P1se = kb.val(se)

            if debug:
                kb.wait("gpsimd", ve, v_bias)
                kb.dma("gpsimd", dbg["qsum"][:, :], qsum[:, :], sdb)
                kb.dma("gpsimd", dbg["mrow"][:, :, :], mrow[:, :, :], sdb)
                kb.dma("gpsimd", dbg["shi"][:, :, :], shi[:, :, :], sdb)
                kb.dma("gpsimd", dbg["statc"][:, :], statc[:, :], sdb)
                v_dbg1 = kb.val(sdb)
                kb.wait("tensor", sdb, v_dbg1)

        for e in ENGINES:
            kb.wait(e, te, v_projdone)
            kb.wait(e, ve, v_bias)
            kb.wait(e, se, v_P1se)

        with contextlib.ExitStack() as p23:
          with contextlib.ExitStack() as p2:
            A2 = p2.enter_context(sb("A2", [128, 4, 2, 512], BF16))
            pb = p2.enter_context(sb("pb", [128, 4, 512], BF16))
            recv = p2.enter_context(sb("recv", [1, 1024], BF16))
            bcsb = p2.enter_context(sb("bcsb", [128, 512], BF16))
            recf = p2.enter_context(sb("recf", [1, 1024], F32))

            woT_bf = p2.enter_context(sb("woT_bf", [128, 4, D], BF16))
            wst2 = p2.enter_context(sb("wst2", [128, 2, D], F32))
            ccst = p2.enter_context(sb("ccst", [128, 2, D], BF16))
            xot = p2.enter_context(sb("xot", [128, 2, D], F32))
            xnf = p2.enter_context(sb("xnf", [128, D], F32))
            r16q = p2.enter_context(sb("r16q", [128, D], FP16))
            jnk2 = p2.enter_context(sb("jnk2", [128, D], F32))

            ps_s = p2.enter_context(ps("ps_s", [128, 4, 512], F32))
            ps_wo = p2.enter_context(ps("ps_wo", [128, 2, 512], F32))
            ps_pv = [p2.enter_context(ps(f"ps_pv{i}", [65, 512], F32)) for i in range(2)]

            kb.op("vector", "engine_nop", inc=ve)

            wofree = {}
            if debug:
                kb.wait("sync", sdb)
            for ch in range(4):
                par = ch % 2
                if par in wofree:
                    kb.wait("sync", ve, wofree[par])
                kb.dma("sync", wst2[:, par, :], woT[128 * ch:128 * (ch + 1), :],
                       swt[par])
                kb.wait("vector", swt[par])
                kb.op("vector", "tensor_copy", woT_bf[:, ch, :],
                      wst2[:, par, :], inc=ve)
                wofree[par] = kb.val(ve)
            v_wo = kb.val(ve)

            sfree = {}
            sbcf = {}
            pbfree = {}
            pvfree = None
            Aev = {}
            woA = {}
            wo_ev = {0: 0, 1: 0}
            ccfree = {}
            chv = {}
            xodma = {}
            qtd = {}
            wo_started = [False]

            def emit_xot_dma(t):
                if t >= 2:
                    kb.wait("sync", ve, chv[t - 2])
                kb.dma("sync", xot[:, t % 2, :],
                       xo[128 * t:128 * (t + 1), :], sxo[t % 2])
                xodma[t] = kb.val(sxo[t % 2])

            def emit_qchain(t):
                at = xot[:, t % 2, :]
                kb.wait("vector", sxo[t % 2], xodma[t])
                kb.sop("vector", "tensor_reduce", smt[:, 0:1], at, AX.X,
                       ALU.add)
                kb.wait("scalar", sxo[t % 2], xodma[t])
                if t >= 1:
                    kb.wait("scalar", te, qtd[t - 1])
                kb.op("scalar", "activation", jnk2[:, :], at, AF.Square,
                      bias=0.0, scale=1.0, accum_out=smt[:, 15:16], inc=se)
                kb.sop("vector", "tensor_scalar", smt[:, 2:3], smt[:, 0:1],
                       1.0 / D, None, ALU.mult)
                kb.wait("vector", se)
                kb.sop("vector", "tensor_scalar", smt[:, 3:4], smt[:, 15:16],
                       1.0 / D, None, ALU.mult)
                kb.sop("vector", "tensor_tensor", smt[:, 4:5], smt[:, 2:3],
                       smt[:, 2:3], ALU.mult)
                kb.sop("vector", "tensor_tensor", smt[:, 5:6], smt[:, 3:4],
                       smt[:, 4:5], ALU.subtract)
                kb.sop("vector", "tensor_scalar", smt[:, 6:7], smt[:, 5:6],
                       EPS_LN, None, ALU.add)
                v_v1 = kb.val(ve)
                kb.wait("scalar", ve, v_v1)
                kb.sop("scalar", "activation", smt[:, 7:8], smt[:, 6:7],
                       AF.Ln, bias=0.0, scale=1.0)
                kb.sop("scalar", "activation", smt[:, 8:9], smt[:, 7:8],
                       AF.Exp, bias=0.0, scale=-0.5)
                kb.wait("vector", se)
                kb.sop("vector", "tensor_scalar", xnf[:, :], at,
                       smt[:, 2:3], smt[:, 8:9], ALU.subtract, op1=ALU.mult)
                kb.sop("vector", "tensor_reduce", smt[:, 9:10], xnf[:, :],
                       AX.X, ALU.add)
                kb.wait("scalar", ve)
                kb.op("scalar", "activation", jnk2[:, :], xnf[:, :], AF.Square,
                      bias=0.0, scale=1.0, accum_out=smt[:, 10:11], inc=se)
                kb.sop("vector", "tensor_scalar", smt[:, 11:12], smt[:, 9:10],
                       1.0 / D, None, ALU.mult)
                kb.wait("vector", se)
                kb.sop("vector", "tensor_scalar", smt[:, 12:13], smt[:, 10:11],
                       1.0 / D, None, ALU.mult)
                kb.sop("vector", "tensor_tensor", smt[:, 13:14], smt[:, 11:12],
                       smt[:, 11:12], ALU.mult)
                kb.sop("vector", "tensor_tensor", smt[:, 12:13], smt[:, 12:13],
                       smt[:, 13:14], ALU.subtract)
                kb.sop("vector", "tensor_scalar", smt[:, 12:13], smt[:, 12:13],
                       EPS_RMS, None, ALU.add)
                v_v2 = kb.val(ve)
                kb.wait("scalar", ve, v_v2)
                kb.sop("scalar", "activation", smt[:, 13:14], smt[:, 12:13],
                       AF.Ln, bias=0.0, scale=1.0)
                kb.sop("scalar", "activation", smt[:, 14:15], smt[:, 13:14],
                       AF.Exp, bias=0.0, scale=-0.5)
                kb.sop("vector", "tensor_scalar", xnf[:, :], xnf[:, :],
                       smt[:, 11:12], None, ALU.subtract)
                kb.sop("vector", "tensor_reduce", smt[:, 0:1], xnf[:, :],
                       AX.X, ALU.max, apply_absolute_value=True)
                kb.sop("vector", "reciprocal", smt[:, 1:2], smt[:, 0:1])
                kb.sop("vector", "tensor_scalar", smt[:, 2:3], smt[:, 1:2],
                       127.0, None, ALU.mult)
                kb.sop("vector", "tensor_scalar", r16q[:, :], xnf[:, :],
                       smt[:, 2:3], MAGIC, ALU.mult, op1=ALU.add)
                kb.wait("vector", se)
                kb.sop("vector", "tensor_tensor", smt[:, 3:4], smt[:, 0:1],
                       smt[:, 14:15], ALU.mult)
                kb.sop("vector", "tensor_scalar", smt[:, 4:5], smt[:, 3:4],
                       1.0 / 127.0, None, ALU.mult)
                if t >= 1:
                    kb.wait("vector", te, qtd[t - 1])
                kb.sop("vector", "tensor_scalar", jnk2[:, :], r16q[:, :],
                       MAGIC, smt[:, 4:5], ALU.subtract, op1=ALU.mult)
                chv[t] = kb.val(ve)

            def emit_qtrans(t):
                kb.wait("tensor", ve, chv[t])
                for qd in range(4):
                    bank = qd % 2
                    kb.wait("tensor", ve, wo_ev[bank])
                    for jj in range(4):
                        j = 4 * qd + jj
                        kb.op("tensor", "transpose",
                              ps_wo[:, bank, 128 * jj:128 * (jj + 1)],
                              jnk2[:, 128 * j:128 * (j + 1)], idnf[:, :],
                              inc=(te if jj == 3 else None))
                    vt = kb.val(te)
                    kb.wait("vector", te, vt)
                    kb.op("vector", "tensor_copy",
                          xqT[:, 4 * qd:4 * (qd + 1), 128 * t:128 * (t + 1)],
                          ps_wo[:, bank, :].rearrange("p (j c) -> p j c", j=4),
                          inc=ve)
                    wo_ev[bank] = kb.val(ve)
                qtd[t] = kb.val(te)

            def emit_wo_tile(qtw, tloc):
                tt_idx = 4 * qtw + tloc
                tp = tt_idx % 2
                if tp in ccfree:
                    kb.wait("vector", scm[tp], ccfree[tp])
                if not wo_started[0]:
                    kb.wait("tensor", ve, v_wo)
                    wo_started[0] = True
                if tloc == 0:
                    kb.wait("tensor", ve, Aev[qtw])
                for dq in range(4):
                    bank = dq % 2
                    kb.wait("tensor", ve, wo_ev[bank])
                    for pcw in range(4):
                        kb.op("tensor", "matmul", ps_wo[:, bank, :],
                              A2[:, pcw, qtw % 2, 128 * tloc:128 * (tloc + 1)],
                              woT_bf[:, pcw, 512 * dq:512 * (dq + 1)],
                              start=(pcw == 0), stop=(pcw == 3),
                              inc=(te if pcw == 3 else None))
                    kb.wait("vector", te)
                    kb.op("vector", "tensor_copy",
                          ccst[:, tp, 512 * dq:512 * (dq + 1)],
                          ps_wo[:, bank, :], inc=ve)
                    wo_ev[bank] = kb.val(ve)
                kb.wait("gpsimd", ve)
                ccfree[tp] = kb.dma("gpsimd",
                                    cc_in.ap()[128 * tt_idx:128 * (tt_idx + 1), :],
                                    ccst[:, tp, :], scm[tp])
                if tloc == 3:
                    woA[qtw] = kb.val(te)

            for qt in range(QTN):
                emit_xot_dma(qt)
                q0 = 512 * qt
                subs = [(kt, ks) for kt in range(qt + 1) for ks in range(4)]
                nsub = len(subs)
                for pc in range(4):
                    stage = {}
                    for i in range(nsub + 1):
                        if i < nsub:
                            par = i % 2
                            kt, ks = subs[i]
                            k0 = 512 * kt + 128 * ks
                            for hh, Kp in enumerate((Kpa, Kpb)):
                                slot = 2 * par + hh
                                if slot in sfree:
                                    kb.wait("tensor", se, sfree[slot])
                                if slot in sbcf:
                                    kb.wait("tensor", ve, sbcf[slot])
                                    del sbcf[slot]
                                kb.op("tensor", "matmul", ps_s[:, slot, :],
                                      Kp[:, k0:k0 + 128], Q[:, pc, q0:q0 + 512],
                                      start=True, stop=True,
                                      inc=(te if hh == 1 else None))
                            stage[i] = kb.val(te)
                            diag = (kt == qt)
                            kb.wait("scalar", te, stage[i])
                            for hh in range(2):
                                slot = 2 * par + hh
                                if slot in pbfree:
                                    kb.wait("scalar", te, pbfree[slot])
                            cs = 128 * ks if diag else 0
                            kb.op("scalar", "activation",
                                  pb[:, 2 * par:2 * par + 2, cs:512],
                                  ps_s[:, 2 * par:2 * par + 2, cs:512],
                                  AF.Exp, bias=biasb[:, pc:pc + 1],
                                  scale=SCALE, inc=se)
                            v_exp = kb.val(se)
                            sfree[2 * par] = v_exp
                            sfree[2 * par + 1] = v_exp
                            v_mask = None
                            if diag:
                                kb.wait("vector", se, v_exp)
                                if ks > 0:
                                    kb.op("vector", "memset",
                                          pb[:, 2 * par:2 * par + 2,
                                             0:128 * ks], 0.0)
                                for hh in range(2):
                                    slot = 2 * par + hh
                                    kb.op("vector", "tensor_tensor",
                                          pb[:, slot, 128 * ks:128 * (ks + 1)],
                                          pb[:, slot, 128 * ks:128 * (ks + 1)],
                                          tri[:, :], ALU.mult,
                                          inc=(ve if hh == 1 else None))
                                v_mask = kb.val(ve)
                            stage[i] = (v_exp, v_mask)
                        if i >= 1:
                            j = i - 1
                            par = j % 2
                            kt, ks = subs[j]
                            ci_ = 4 * kt + ks
                            v_exp, v_mask = stage[j]
                            if j == 0 and pvfree is not None:
                                kb.wait("tensor", ve, pvfree)
                            if v_mask is not None:
                                kb.wait("tensor", ve, v_mask)
                            else:
                                kb.wait("tensor", se, v_exp)
                            for hh, Vx in enumerate((Va, Vb)):
                                slot = 2 * par + hh
                                kb.op("tensor", "matmul", ps_pv[hh][:, :],
                                      Vx[:, ci_, 0:65], pb[:, slot, :],
                                      start=(j == 0), stop=(j == nsub - 1),
                                      inc=(te if hh == 1 else None))
                            v_pvp = kb.val(te)
                            pbfree[2 * par] = v_pvp
                            pbfree[2 * par + 1] = v_pvp
                    kb.wait("vector", te, v_pvp)
                    if pc == 0 and qt >= 2:
                        kb.wait("vector", te, woA[qt - 2])
                    kb.sop("vector", "tensor_copy",
                           A2[0:64, pc, qt % 2, :], ps_pv[0][0:64, :])
                    kb.sop("vector", "tensor_copy",
                           A2[64:128, pc, qt % 2, :], ps_pv[1][0:64, :])
                    kb.sop("vector", "reciprocal", recf[0:1, 0:512],
                           ps_pv[0][64:65, :])
                    kb.sop("vector", "reciprocal", recf[0:1, 512:1024],
                           ps_pv[1][64:65, :])
                    pvfree = kb.val(ve)
                    kb.sop("vector", "tensor_copy", recv[:, :], recf[:, :])
                    v_rec = kb.val(ve)
                    bcs = 2 * (nsub % 2)
                    for bslot in (bcs, bcs + 1):
                        if bslot in sfree:
                            kb.wait("tensor", se, sfree[bslot])
                        if bslot in sbcf:
                            kb.wait("tensor", ve, sbcf[bslot])
                            del sbcf[bslot]
                    kb.wait("tensor", ve, v_rec)
                    kb.op("tensor", "matmul", ps_s[0:64, bcs, :],
                          onesrow[0:1, 0:64], recv[:, 0:512], start=True,
                          stop=True)
                    kb.op("tensor", "matmul", ps_s[0:64, bcs + 1, :],
                          onesrow[0:1, 0:64], recv[:, 512:1024], start=True,
                          stop=True, inc=te)
                    v_bc = kb.val(te)
                    kb.wait("vector", te, v_bc)
                    kb.sop("vector", "tensor_copy", bcsb[0:64, :],
                           ps_s[0:64, bcs, :])
                    kb.sop("vector", "tensor_copy", bcsb[64:128, :],
                           ps_s[0:64, bcs + 1, :])
                    kb.wait("vector", ve)
                    for hh in range(2):
                        kb.op("vector", "tensor_tensor",
                              A2[64 * hh:64 * (hh + 1), pc, qt % 2, :],
                              A2[64 * hh:64 * (hh + 1), pc, qt % 2, :],
                              bcsb[64 * hh:64 * (hh + 1), :], ALU.mult,
                              inc=(ve if hh == 1 else None))
                    v_scale = kb.val(ve)
                    sbcf[bcs] = v_scale
                    sbcf[bcs + 1] = v_scale
                    if pc == 3:
                        Aev[qt] = v_scale
                    if qt >= 1:
                        emit_wo_tile(qt - 1, pc)
                        if pc == 0:
                            emit_qchain(qt - 1)
                        if pc == 2:
                            emit_qtrans(qt - 1)
                        if pc == 3 and qt == QTN - 1:
                            emit_qchain(QTN - 1)

            for tloc in range(4):
                emit_wo_tile(QTN - 1, tloc)
            emit_qtrans(QTN - 1)
            v_woD = kb.val(te)
            v_xqT = kb.val(ve)
            v_P2se = kb.val(se)
            v_P2ve = kb.val(ve)

            if debug:
                kb.wait("gpsimd", ve, v_xqT)
                kb.dma("gpsimd", dbg["A"][:, :, 0:1024],
                       A2[:, :, :, :].rearrange("p a b c -> p a (b c)"), sdb)
                kb.dma("gpsimd", dbg["biasb"][:, :], biasb[:, :], sdb)
                kb.dma("gpsimd", dbg["Q"][:, :, :], Q[:, :, :], sdb)
                kb.dma("gpsimd", dbg["Kpa"][:, :], Kpa[:, :], sdb)
                kb.dma("gpsimd", dbg["Kpb"][:, :], Kpb[:, :], sdb)
                kb.dma("gpsimd", dbg["Va"][:, :, :], Va[:, :, :], sdb)
                kb.dma("gpsimd", dbg["xqT"][:, :, :], xqT[:, :, :], sdb)

          kb.wait("gpsimd", scm[0])
          kb.wait("gpsimd", scm[1])
          kb.op("gpsimd", "collective_compute", "ReduceScatter", ALU.add,
                replica_groups=[[0, 1, 2, 3], [4, 5, 6, 7]],
                ins=[cc_in.ap().opt()], outs=[cc_out.ap().opt()], inc=scc)

          es_attn.close()
          with contextlib.ExitStack() as p3:
            acc = p3.enter_context(sb("acc", [128, 8, D], F32))
            r16t = p3.enter_context(sb("r16t", [128, D], FP16))
            wgf = p3.enter_context(sb("wgf", [128, 2, 16, 128], F32))
            wuf = p3.enter_context(sb("wuf", [128, 2, 16, 128], F32))
            wdf = p3.enter_context(sb("wdf", [128, 2, 2048], F32))
            wg_s = p3.enter_context(sb("wg_s", [128, 3, 16, 128], FP8))
            wu_s = p3.enter_context(sb("wu_s", [128, 3, 16, 128], FP8))
            wd_s = p3.enter_context(sb("wd_s", [128, 8, D], FP8))
            p_ring = p3.enter_context(sb("p_ring", [128, 8, OWN], BF16))
            sil = p3.enter_context(sb("sil", [128, 2, 512], BF16))
            rsb = p3.enter_context(sb("rsb", [128, D], BF16))
            jnk3 = rsb

            ps_g = [p3.enter_context(ps(f"ps_g{i}", [128, 512], F32)) for i in range(2)]
            ps_u = [p3.enter_context(ps(f"ps_u{i}", [128, 512], F32)) for i in range(2)]
            ps_o = [p3.enter_context(ps(f"ps_o{i}", [128, 512], F32)) for i in range(2)]

            kb.wait("sync", te, v_woD)
            kb.wait("sync", scm[0])
            kb.wait("sync", scm[1])
            kb.wait("sync", ve, v_P2ve)
            kb.wait("sync", se, v_P2se)
            if debug:
                kb.wait("sync", sdb)
            for t in range(8):
                kb.dma("sync", acc[:, t, :], xo[128 * t:128 * (t + 1), :], sac)
            v_acc0 = kb.val(sac)
            kb.wait("vector", sac, v_acc0)

            kb.wait("tensor", ve, v_xqT)
            kb.wait("scalar", te, v_woD)
            kb.wait("scalar", ve, v_P2ve)
            if debug:
                kb.wait("gpsimd", sdb)
            sgn_gu = {}
            wdsv = {}
            wddma = {}
            gu_done = {}
            wd_done = {}
            gfree = {}
            ufree = {}
            o_free = {}
            silfree = {}

            def emit_wdma(hc):
                par = hc % 2
                if hc >= 2:
                    kb.wait("scalar", se, sgn_gu[hc - 2])
                    kb.wait("sync", se, sgn_gu[hc - 2])
                kb.dma("scalar", wgf[:, par, :, :],
                       wgT[hc, :, :].rearrange("p (j h) -> p j h", j=16),
                       swg[par])
                kb.dma("sync", wuf[:, par, :, :],
                       wuT[hc, :, :].rearrange("p (j h) -> p j h", j=16),
                       swu[par])
                if hc >= 2:
                    kb.wait("sync", ve, wdsv[hc - 2])
                kb.dma("sync", wdf[:, par, :],
                       wdT[128 * hc:128 * (hc + 1), :], swd[par])
                wddma[hc] = kb.val(swd[par])

            def emit_signs(hc):
                par = hc % 2
                s4 = hc % 3
                kb.wait("scalar", swg[par], kb.val(swg[par]))
                if hc >= 3:
                    kb.wait("scalar", te, gu_done[hc - 3])
                kb.op("scalar", "activation", wg_s[:, s4, :, :],
                      wgf[:, par, :, :], AF.Sign, bias=0.0, scale=1.0)
                kb.wait("scalar", swu[par], kb.val(swu[par]))
                kb.op("scalar", "activation", wu_s[:, s4, :, :],
                      wuf[:, par, :, :], AF.Sign, bias=0.0, scale=1.0, inc=se)
                sgn_gu[hc] = kb.val(se)

            def emit_compute(hc):
                par = hc % 2
                slot8 = hc % 8
                s4 = hc % 3
                kb.wait("tensor", se, sgn_gu[hc])
                for th in range(2):
                    gb = th
                    if gb in gfree:
                        kb.wait("tensor", se, gfree[gb])
                        kb.wait("tensor", ve, ufree[gb])
                    for j in range(16):
                        kb.op("tensor", "matmul", ps_g[gb][:, :],
                              wg_s[:, s4, j, :],
                              xqT[:, j, 512 * th:512 * (th + 1)],
                              start=(j == 0), stop=(j == 15),
                              inc=(te if j == 15 else None))
                    for j in range(16):
                        kb.op("tensor", "matmul", ps_u[gb][:, :],
                              wu_s[:, s4, j, :],
                              xqT[:, j, 512 * th:512 * (th + 1)],
                              start=(j == 0), stop=(j == 15),
                              inc=(te if j == 15 else None))
                    v_gu = kb.val(te)
                    kb.wait("scalar", te, v_gu)
                    if gb in silfree:
                        kb.wait("scalar", ve, silfree[gb])
                    kb.op("scalar", "activation", sil[:, gb, :],
                          ps_g[gb][:, :], AF.Silu, bias=0.0, scale=1.0, inc=se)
                    v_sil = kb.val(se)
                    gfree[gb] = v_sil
                    kb.wait("vector", se, v_sil)
                    if hc >= 8 and th == 0:
                        kb.wait("vector", te, wd_done[hc // 8 - 1])
                    kb.op("vector", "tensor_tensor",
                          p_ring[:, slot8, 512 * th:512 * (th + 1)],
                          sil[:, gb, :], ps_u[gb][:, :], ALU.mult, inc=ve)
                    silfree[gb] = kb.val(ve)
                    ufree[gb] = kb.val(ve)
                gu_done[hc] = kb.val(te)
                kb.wait("vector", swd[par], wddma[hc])
                if hc >= 8:
                    kb.wait("vector", te, wd_done[hc // 8 - 1])
                kb.sop("vector", "tensor_scalar", jnk3[:, :], wdf[:, par, :],
                       0.0, None, ALU.is_gt)
                kb.sop("vector", "tensor_scalar", r16t[:, :], wdf[:, par, :],
                       0.0, None, ALU.is_lt)
                kb.sop("vector", "tensor_tensor", wd_s[:, slot8, :],
                       jnk3[:, :], r16t[:, :], ALU.subtract)
                wdsv[hc] = kb.val(ve)
                if slot8 == 7:
                    kb.wait("tensor", ve, wdsv[hc])
                    for t8 in range(8):
                        for dq in range(4):
                            ob = (t8 * 4 + dq) % 2
                            if ob in o_free:
                                kb.wait("tensor", ve, o_free[ob])
                            for h8 in range(8):
                                kb.op("tensor", "matmul", ps_o[ob][:, :],
                                      p_ring[:, h8, 128 * t8:128 * (t8 + 1)],
                                      wd_s[:, h8, 512 * dq:512 * (dq + 1)],
                                      start=(h8 == 0), stop=(h8 == 7),
                                      inc=(te if h8 == 7 else None))
                            v_od = kb.val(te)
                            kb.wait("vector", te, v_od)
                            kb.op("vector", "tensor_tensor",
                                  acc[:, t8, 512 * dq:512 * (dq + 1)],
                                  acc[:, t8, 512 * dq:512 * (dq + 1)],
                                  ps_o[ob][:, :], ALU.add, inc=ve)
                            o_free[ob] = kb.val(ve)
                    wd_done[hc // 8] = kb.val(te)

            for it in range(66):
                if it >= 2:
                    emit_compute(it - 2)
                if it < 64:
                    emit_wdma(it)
                    emit_signs(it)

            accflush = None
            kb.wait("gpsimd", scc, 1)
            kb.wait("gpsimd", ve, wdsv[63])
            for t in range(8):
                if accflush is not None:
                    kb.wait("gpsimd", ve, accflush)
                v_rs = kb.dma("gpsimd", rsb[:, :],
                              cc_out.ap()[128 * t:128 * (t + 1), :], scm[0])
                kb.wait("vector", scm[0], v_rs)
                kb.op("vector", "tensor_tensor", acc[:, t, :], acc[:, t, :],
                      rsb[:, :], ALU.add, inc=ve)
                accflush = kb.val(ve)
            v_out = kb.val(ve)
            kb.wait("gpsimd", ve, v_out)
            for t in range(8):
                kb.dma("gpsimd", y[128 * t:128 * (t + 1), :], acc[:, t, :], syy)
            kb.wait("gpsimd", syy)

        with nc.Block() as block:
            @block.sync
            def _(e):
                for f in kb.streams["sync"]:
                    f(e)

            @block.tensor
            def _(e):
                for f in kb.streams["tensor"]:
                    f(e)

            @block.vector
            def _(e):
                for f in kb.streams["vector"]:
                    f(e)

            @block.scalar
            def _(e):
                for f in kb.streams["scalar"]:
                    f(e)

            @block.gpsimd
            def _(e):
                for f in kb.streams["gpsimd"]:
                    f(e)
    return nc


_SPLIT = np.concatenate([np.arange(0, 64, 2), np.arange(1, 64, 2)])


def _prep(inputs):
    x = np.ascontiguousarray(np.asarray(inputs["L0_attn_x"], dtype=np.float32))
    Wq = np.asarray(inputs["L0_attn_Wq"], dtype=np.float32)
    Wk = np.asarray(inputs["L0_attn_Wk"], dtype=np.float32)
    Wv = np.asarray(inputs["L0_attn_Wv"], dtype=np.float32)
    Wo = np.asarray(inputs["L0_attn_Wo"], dtype=np.float32)
    Wg = np.asarray(inputs["L0_ffn_W_g"], dtype=np.float32)
    Wu = np.asarray(inputs["L0_ffn_W_u"], dtype=np.float32)
    Wd = np.asarray(inputs["L0_ffn_W_d"], dtype=np.float32)

    wgT = np.ascontiguousarray(
        Wg.T.reshape(16, 128, 64, 128).transpose(2, 1, 0, 3).reshape(64, 128, 2048))
    wuT = np.ascontiguousarray(
        Wu.T.reshape(16, 128, 64, 128).transpose(2, 1, 0, 3).reshape(64, 128, 2048))
    wdT = np.ascontiguousarray(Wd.T)

    inv = 10000.0 ** (-np.arange(0, HD, 2, dtype=np.float64) / HD)
    pos = np.arange(L, dtype=np.float64)
    f = inv[:, None] * pos[None, :]
    c = np.cos(f).astype(np.float32)
    s = np.sin(f).astype(np.float32)
    CC = np.tile(c, (4, 1))
    SS = np.concatenate([-s, s, -s, s], axis=0)
    cs = np.ascontiguousarray(np.stack([CC, SS]))

    tri = np.triu(np.ones((128, 128), dtype=np.float32)).astype(bfdt)
    idn = np.eye(128, dtype=bfdt)
    idf = np.eye(128, dtype=np.float32)
    onc = np.ones((128, 1), dtype=bfdt)
    onr = np.ones((1, 128), dtype=bfdt)

    xT = [np.ascontiguousarray(
        x[b].T.reshape(4, 4, 128, 8, 512).transpose(3, 0, 2, 1, 4)
        .reshape(32, 128, 2048)) for b in range(B)]

    maps = []
    for c_ in range(8):
        b, j = c_ // 4, c_ % 4
        qheads = [8 * j + 4 * hh + pc for pc in range(4) for hh in range(2)]
        qrows = np.concatenate([64 * h + _SPLIT for h in qheads])
        wqT_c = np.ascontiguousarray(
            Wq[qrows].T.reshape(16, 128, 4, 128).transpose(2, 1, 0, 3)
            .reshape(4, 128, 2048))
        krows = np.concatenate([64 * (2 * j) + _SPLIT, 64 * (2 * j + 1) + _SPLIT])
        wkT_c = np.ascontiguousarray(
            Wk[krows].T.reshape(16, 128, 128).transpose(1, 0, 2)
            .reshape(1, 128, 2048))
        vrows = np.arange(64 * 2 * j, 64 * (2 * j + 2))
        wvT_c = np.ascontiguousarray(
            Wv[vrows].T.reshape(16, 128, 128).transpose(1, 0, 2)
            .reshape(1, 128, 2048))
        orows = np.concatenate([np.arange(64 * h, 64 * (h + 1)) for h in qheads])
        woT_c = np.ascontiguousarray(Wo.T[orows])
        maps.append({
            "xfT": xT[b],
            "xo": np.ascontiguousarray(x[b, 1024 * j:1024 * (j + 1)]),
            "wqT": wqT_c, "wkT": wkT_c, "wvT": wvT_c, "woT": woT_c,
            "wgT": wgT, "wuT": wuT, "wdT": wdT,
            "cs_tab": cs, "tri_in": tri, "idn_in": idn,
            "idf_in": idf,
            "onc_in": onc, "onr_in": onr,
        })
    return maps


_NC_CACHE = {}


def kernel(**inputs):
    if "nc" not in _NC_CACHE:
        _NC_CACHE["nc"] = build_nc(debug=False)
    nc = _NC_CACHE["nc"]
    maps = _prep(inputs)
    res = run_bass_kernel_spmd(nc, maps, core_ids=list(range(8)))
    out = np.empty((B, L, D), dtype=np.float32)
    for c_ in range(8):
        b, j = c_ // 4, c_ % 4
        out[b, 1024 * j:1024 * (j + 1)] = res.results[c_]["y"]
    return out


# version 19 (final)
# speedup vs baseline: 1.0675x; 1.0675x over previous
import contextlib
import numpy as np
import ml_dtypes
import concourse.bass as bass
import concourse.mybir as mybir
from concourse.bass_utils import run_bass_kernel_spmd

F32 = mybir.dt.float32
BF16 = mybir.dt.bfloat16
FP16 = mybir.dt.float16
FP8 = mybir.dt.float8e4
AF = mybir.ActivationFunctionType
ALU = mybir.AluOpType
AX = mybir.AxisListType
bfdt = ml_dtypes.bfloat16

B, L, D = 2, 4096, 2048
HQ, HKV, HD = 32, 8, 64
H = 8192
NT = L // 128
NGQ = L // 512
QTN = L // 512
OWN = 1024
SCALE = 0.125
MAGIC = 1536.0
EPS_LN = 1e-5
EPS_RMS = 1e-8

ENGINES = ("sync", "tensor", "vector", "scalar", "gpsimd")


class KB:

    def __init__(self, nc, es):
        self.nc = nc
        self.streams = {e: [] for e in ENGINES}
        self.sems = {}
        self.cnt = {}
        self.es = es

    def sem(self, name):
        self.sems[name] = self.es.enter_context(self.nc.semaphore(name))
        self.cnt[name] = 0
        return name

    def val(self, s):
        return self.cnt[s]

    def op(self, eng, meth, *args, inc=None, inck=1, **kw):
        h = self.sems[inc] if inc else None
        if inc:
            self.cnt[inc] += inck
        post = self.cnt[inc] if inc else None

        def emit(e, meth=meth, args=args, kw=kw, h=h, inck=inck):
            r = getattr(e, meth)(*args, **kw)
            if h is not None:
                r.then_inc(h, inck)
        self.streams[eng].append(emit)
        return post

    def wait(self, eng, s, v=None):
        v = self.cnt[s] if v is None else v
        h = self.sems[s]

        def emit(e, h=h, v=v):
            e.wait_ge(h, v)
        self.streams[eng].append(emit)

    def dma(self, eng, out, in_, inc, **kw):
        return self.op(eng, "dma_start", out=out, in_=in_, inc=inc, inck=16, **kw)

    def sop(self, eng, meth, *args, **kw):
        s = {"vector": "ve", "scalar": "se"}[eng]
        self.wait(eng, s)
        return self.op(eng, meth, *args, inc=s, **kw)


def build_nc(debug=False, detect_races=False):
    nc = bass.Bass("TRN2", target_bir_lowering=False,
                   detect_race_conditions=detect_races)

    xfT = nc.dram_tensor("xfT", [32, 128, 2048], F32, kind="ExternalInput").ap()
    xo = nc.dram_tensor("xo", [OWN, D], F32, kind="ExternalInput").ap()
    wqT = nc.dram_tensor("wqT", [4, 128, 2048], F32, kind="ExternalInput").ap()
    wkT = nc.dram_tensor("wkT", [1, 128, 2048], F32, kind="ExternalInput").ap()
    wvT = nc.dram_tensor("wvT", [1, 128, 2048], F32, kind="ExternalInput").ap()
    woT = nc.dram_tensor("woT", [512, D], F32, kind="ExternalInput").ap()
    wgT = nc.dram_tensor("wgT", [64, 128, 2048], F32, kind="ExternalInput").ap()
    wuT = nc.dram_tensor("wuT", [64, 128, 2048], F32, kind="ExternalInput").ap()
    wdT = nc.dram_tensor("wdT", [H, D], F32, kind="ExternalInput").ap()
    cs_tab = nc.dram_tensor("cs_tab", [2, 128, L], F32, kind="ExternalInput").ap()
    tri_in = nc.dram_tensor("tri_in", [128, 128], BF16, kind="ExternalInput").ap()
    idn_in = nc.dram_tensor("idn_in", [128, 128], BF16, kind="ExternalInput").ap()
    onc_in = nc.dram_tensor("onc_in", [128, 1], BF16, kind="ExternalInput").ap()
    onr_in = nc.dram_tensor("onr_in", [1, 128], BF16, kind="ExternalInput").ap()
    y = nc.dram_tensor("y", [OWN, D], F32, kind="ExternalOutput").ap()
    cc_in = nc.dram_tensor("cc_in", [L, D], BF16)
    cc_out = nc.dram_tensor("cc_out", [OWN, D], BF16)
    dbg = {}
    if debug:
        def dout(name, shape, dt=F32):
            dbg[name] = nc.dram_tensor("dbg_" + name, shape, dt,
                                       kind="ExternalOutput").ap()
        dout("Q", [128, 4, L])
        dout("Kpa", [128, L])
        dout("Kpb", [128, L])
        dout("Va", [128, NT, 97])
        dout("A", [128, 4, L])
        dout("sigm", [64, 512])
        dout("biasb", [128, 4])
        dout("xqT", [128, 16, OWN])
        dout("qsum", [1, 512])
        dout("mrow", [1, 2, 512])
        dout("shi", [1, 2, 512])
        dout("statc", [128, 32])

    with contextlib.ExitStack() as es:
        kb = KB(nc, es)
        te = kb.sem("te")
        ve = kb.sem("ve")
        se = kb.sem("se")
        scc = kb.sem("scc")
        sc0 = kb.sem("sc0")
        sxs = [kb.sem(f"sxs{i}") for i in range(4)]
        stb = [kb.sem(f"stb{i}") for i in range(2)]
        swt = [kb.sem(f"swt{i}") for i in range(2)]
        ssst = [kb.sem(f"sss{i}") for i in range(8)]
        srb = [kb.sem(f"srb{i}") for i in range(4)]
        scm = [kb.sem(f"scm{i}") for i in range(2)]
        sac = kb.sem("sac")
        swg = [kb.sem(f"swg{i}") for i in range(2)]
        swu = [kb.sem(f"swu{i}") for i in range(2)]
        swd = [kb.sem(f"swd{i}") for i in range(2)]
        syy = kb.sem("syy")
        sdb = kb.sem("sdb")

        sb = nc.sbuf_tensor
        ps = nc.psum_tensor
        tri = es.enter_context(sb("tri", [128, 128], BF16))
        idn = es.enter_context(sb("idn", [128, 128], BF16))
        onescol = es.enter_context(sb("onescol", [128, 1], BF16))
        onesrow = es.enter_context(sb("onesrow", [1, 128], BF16))
        biasb = es.enter_context(sb("biasb", [128, 4], F32))
        q2run = es.enter_context(sb("q2run", [1, 4], F32))
        k2run = es.enter_context(sb("k2run", [1, 2], F32))
        smt = es.enter_context(sb("smt", [128, 16], F32))
        s1t = es.enter_context(sb("s1t", [1, 16], F32))
        nbb = es.enter_context(sb("nbb", [1, 4], BF16))
        statc = es.enter_context(sb("statc", [128, 32], F32))
        es_attn = es.enter_context(contextlib.ExitStack())
        Q = es_attn.enter_context(sb("Q", [128, 4, L], BF16))
        Kpa = es_attn.enter_context(sb("Kpa", [128, L], BF16))
        Kpb = es_attn.enter_context(sb("Kpb", [128, L], BF16))
        Va = es_attn.enter_context(sb("Va", [128, NT, 97], BF16))
        Vb = es_attn.enter_context(sb("Vb", [128, NT, 97], BF16))

        kb.dma("sync", tri[:, :], tri_in[:, :], sc0)
        kb.dma("sync", idn[:, :], idn_in[:, :], sc0)
        kb.dma("sync", onescol[:, :], onc_in[:, :], sc0)
        kb.dma("sync", onesrow[:, :], onr_in[:, :], sc0)
        kb.wait("vector", sc0)
        kb.op("vector", "memset", q2run[:, :], 0.0)
        kb.op("vector", "memset", k2run[:, :], 0.0, inc=ve)

        with contextlib.ExitStack() as p1:
            wq_bf = p1.enter_context(sb("wq_bf", [128, 16, 512], BF16))
            wk_bf = p1.enter_context(sb("wk_bf", [128, 16, 128], BF16))
            wv_bf = p1.enter_context(sb("wv_bf", [128, 16, 128], BF16))
            qsum = p1.enter_context(sb("qsum", [1, 512], BF16))
            ksum = p1.enter_context(sb("ksum", [1, 128], BF16))
            vsum = p1.enter_context(sb("vsum", [1, 128], BF16))
            mrow = p1.enter_context(sb("mrow", [1, 2, 512], BF16))
            shi = p1.enter_context(sb("shi", [1, 2, 512], BF16))
            slo = p1.enter_context(sb("slo", [1, 2, 512], BF16))
            m2s = p1.enter_context(sb("m2s", [1, 512], F32))
            vrs = p1.enter_context(sb("vrs", [1, 512], F32))
            rcs = p1.enter_context(sb("rcs", [1, 512], F32))
            tsf = p1.enter_context(sb("tsf", [1, 512], F32))
            xst = p1.enter_context(sb("xst", [128, 4, 2048], F32))
            xb16 = p1.enter_context(sb("xb16", [128, 2, 16, 512], BF16))
            sqb = p1.enter_context(sb("sqb", [128, 4, 4, 512], BF16))
            tabb = p1.enter_context(sb("tabb", [128, 2, 2, 512], F32))
            tab_s = p1.enter_context(sb("tab_s", [128, 2, 2, 512], BF16))
            Qc = p1.enter_context(sb("Qc", [128, 2, 512], BF16))
            rtm = p1.enter_context(sb("rtm", [128, 2, 512], BF16))
            rt1 = p1.enter_context(sb("rt1", [128, 2, 512], BF16))
            rt2 = p1.enter_context(sb("rt2", [128, 2, 512], BF16))
            sqs = p1.enter_context(sb("sqs", [128, 6, 512], BF16))

            ps_q = [p1.enter_context(ps(f"ps_q{i}", [128, 512], F32)) for i in range(4)]
            ps_k = p1.enter_context(ps("ps_k", [128, 512], F32))
            ps_v = p1.enter_context(ps("ps_v", [128, 4, 128], F32))
            ps_misc = p1.enter_context(ps("ps_misc", [128, 512], F32))
            ps_bc = p1.enter_context(ps("ps_bc", [128, 512], F32))

            stage_free = {}
            ci = 0
            for wt, nchunk, dst in ((wqT, 4, wq_bf), (wkT, 1, wk_bf), (wvT, 1, wv_bf)):
                for ch in range(nchunk):
                    slot = ci % 2
                    if ci >= 2:
                        kb.wait("sync", se, stage_free[ci - 2])
                    kb.dma("sync", xst[:, slot, :], wt[ch, :, :], swt[slot])
                    kb.wait("scalar", swt[slot])
                    kb.op("scalar", "copy", dst[:, :, 128 * ch:128 * (ch + 1)],
                          xst[:, slot, :].rearrange("p (j c) -> p j c", j=16),
                          inc=se)
                    stage_free[ci] = kb.val(se)
                    ci += 1

            kb.op("vector", "memset", Kpa[64:128, :], 0.0)
            kb.op("vector", "memset", Kpb[0:64, :], 0.0)
            kb.op("vector", "memset", Va[:, :, 64:97], 0.0)
            kb.op("vector", "memset", Vb[:, :, 64:97], 0.0)
            kb.op("vector", "memset", Va[:, :, 96:97], 1.0)
            kb.op("vector", "memset", Vb[:, :, 96:97], 1.0, inc=ve)
            v_init = kb.val(ve)
            kb.wait("tensor", ve, v_init)
            kb.wait("tensor", se)

            for r, (wt_bf, ncol) in enumerate(((wq_bf, 512), (wk_bf, 128),
                                               (wv_bf, 128))):
                for j in range(16):
                    kb.op("tensor", "matmul",
                          ps_misc[32 * r:32 * r + 1, 0:ncol],
                          onescol[:, :], wt_bf[:, j, :],
                          start=(j == 0), stop=(j == 15),
                          inc=(te if j == 15 else None))
            v_wsum = kb.val(te)
            kb.wait("vector", te, v_wsum)
            kb.op("vector", "tensor_copy", qsum[:, :], ps_misc[0:1, 0:512])
            kb.op("vector", "tensor_copy", ksum[:, :], ps_misc[32:33, 0:128])
            kb.op("vector", "tensor_copy", vsum[:, :], ps_misc[64:65, 0:128],
                  inc=ve)
            v_wsum_ev = kb.val(ve)
            kb.wait("tensor", ve, v_wsum_ev)

            cast_done = {}
            sq_done = {}
            xb_free = {}
            chain_read = {}
            evac_done = {}
            corr_done = {}
            trans_done = {}
            stats_done = {}
            vdone = {}
            ropev = {}
            sqrt_done = {}
            tabfree = {}
            statq = {}
            sqs_free = {}
            boundv = {}
            PROJ = ["q0", "q1", "q2", "q3", "k"]

            brow_free = {0: v_wsum_ev, 64: v_wsum_ev}

            def emit_bound(gb):
                pos0b = 512 * gb
                kb.wait("scalar", ve, ropev[gb])
                if "sq" in sqs_free:
                    kb.wait("scalar", te, sqs_free["sq"])
                kb.op("scalar", "activation", sqs[:, 0:4, :],
                      Q[:, :, pos0b:pos0b + 512], AF.Square,
                      bias=0.0, scale=1.0)
                kb.op("scalar", "activation", sqs[:, 4, :],
                      Kpa[:, pos0b:pos0b + 512], AF.Square,
                      bias=0.0, scale=1.0)
                kb.op("scalar", "activation", sqs[:, 5, :],
                      Kpb[:, pos0b:pos0b + 512], AF.Square,
                      bias=0.0, scale=1.0, inc=se)
                v_sq = kb.val(se)
                kb.wait("tensor", se, v_sq)
                jobs = [(q2run, pc, pc) for pc in range(4)]
                jobs += [(k2run, 4, 0), (k2run, 5, 1)]
                for idx, (run, sslot, runslot) in enumerate(jobs):
                    row = 64 if idx % 2 == 0 else 0
                    if row == 0 and idx == 1:
                        kb.wait("tensor", ve, chain_read[gb + 1]
                                if gb + 1 in chain_read else chain_read[gb])
                    kb.wait("tensor", ve, brow_free[row])
                    kb.op("tensor", "matmul", ps_misc[row:row + 1, :],
                          onescol[:, :], sqs[:, sslot, :], start=True,
                          stop=True, inc=te)
                    kb.wait("vector", te)
                    kb.sop("vector", "tensor_reduce", s1t[:, 0:1],
                           ps_misc[row:row + 1, :], AX.X, ALU.max)
                    kb.sop("vector", "tensor_tensor",
                           run[:, runslot:runslot + 1],
                           run[:, runslot:runslot + 1], s1t[:, 0:1], ALU.max)
                    brow_free[row] = kb.val(ve)
                sqs_free["sq"] = kb.val(te)
                boundv[gb] = kb.val(ve)

            for g in range(NGQ):
                par = g % 2
                pos0 = 512 * g
                for s in range(4):
                    cig = 4 * g + s
                    slot = cig % 4
                    q = "sync" if cig % 2 == 0 else "gpsimd"
                    if cig == 0:
                        kb.wait(q, se, stage_free[4])
                    elif cig == 1:
                        kb.wait(q, se, stage_free[5])
                    if cig >= 4:
                        kb.wait(q, se, cast_done[cig - 4])
                    kb.dma(q, xst[:, slot, :], xfT[cig, :, :], sxs[slot])
                if g >= 2:
                    kb.wait("sync", ve, tabfree[par])
                kb.dma("sync", tabb[:, par, 0, :],
                       cs_tab[0, :, pos0:pos0 + 512], stb[par])
                kb.dma("sync", tabb[:, par, 1, :],
                       cs_tab[1, :, pos0:pos0 + 512], stb[par])
                v_tab = kb.val(stb[par])

                for s in range(4):
                    cig = 4 * g + s
                    slot = cig % 4
                    kb.wait("scalar", sxs[slot], kb.val(sxs[slot]))
                    if s == 0 and par in xb_free:
                        kb.wait("scalar", te, xb_free[par])
                        if 4 * (g - 2) + 3 in sq_done:
                            kb.wait("scalar", ve, sq_done[4 * (g - 2) + 3])
                    kb.op("scalar", "copy", xb16[:, par, 4 * s:4 * (s + 1), :],
                          xst[:, slot, :].rearrange("p (j c) -> p j c", j=4),
                          inc=se)
                    cast_done[cig] = kb.val(se)
                    kb.wait("vector", se, cast_done[cig])
                    if s == 0 and g >= 1:
                        kb.wait("vector", te, stats_done[g - 1])
                    kb.op("vector", "tensor_tensor", sqb[:, s, :, :],
                          xb16[:, par, 4 * s:4 * (s + 1), :],
                          xb16[:, par, 4 * s:4 * (s + 1), :], ALU.mult, inc=ve)
                    sq_done[cig] = kb.val(ve)

                if g >= 1:
                    kb.wait("tensor", ve, chain_read[g - 1])
                if g >= 2:
                    kb.wait("tensor", ve, boundv[g - 2])
                for s in range(4):
                    cig = 4 * g + s
                    kb.wait("tensor", se, cast_done[cig])
                    for jj in range(4):
                        j = 4 * s + jj
                        kb.op("tensor", "matmul", ps_misc[0:1, :],
                              onescol[:, :], xb16[:, par, j, :],
                              start=(j == 0), stop=(j == 15))
                    for jj in range(4):
                        j = 4 * s + jj
                        if j == 0:
                            for pcp in range(4):
                                if (g - 1, f"q{pcp}") in evac_done:
                                    kb.wait("tensor", ve,
                                            evac_done[(g - 1, f"q{pcp}")])
                            if (g - 1, "k") in evac_done:
                                kb.wait("tensor", ve, evac_done[(g - 1, "k")])
                        for pcp in range(4):
                            kb.op("tensor", "matmul", ps_q[pcp][:, :],
                                  wq_bf[:, j, 128 * pcp:128 * (pcp + 1)],
                                  xb16[:, par, j, :], start=(j == 0),
                                  stop=False)
                        kb.op("tensor", "matmul", ps_k[:, :], wk_bf[:, j, :],
                              xb16[:, par, j, :], start=(j == 0), stop=False)
                    kb.wait("tensor", ve, sq_done[cig])
                    for jj in range(4):
                        j = 4 * s + jj
                        kb.op("tensor", "matmul", ps_misc[32:33, :],
                              onescol[:, :], sqb[:, s, jj, :],
                              start=(j == 0), stop=(j == 15),
                              inc=(te if j == 15 else None))
                stats_done[g] = kb.val(te)

                kb.wait("vector", te, stats_done[g])
                if g >= 2:
                    kb.wait("vector", te, vdone[g - 2])
                    kb.wait("vector", te, trans_done[g - 2])
                    kb.wait("vector", se, sqrt_done[g - 1])
                kb.sop("vector", "tensor_scalar", mrow[:, par, :],
                       ps_misc[0:1, :], -1.0 / D, None, ALU.mult)
                kb.sop("vector", "tensor_tensor", m2s[:, :], mrow[:, par, :],
                       mrow[:, par, :], ALU.mult)
                kb.sop("vector", "tensor_scalar", vrs[:, :],
                       ps_misc[32:33, :], 1.0 / D, EPS_LN, ALU.mult,
                       op1=ALU.add)
                kb.sop("vector", "tensor_tensor", vrs[:, :], vrs[:, :],
                       m2s[:, :], ALU.subtract)
                kb.sop("vector", "reciprocal", rcs[:, :], vrs[:, :])
                chain_read[g] = kb.val(ve)
                kb.wait("scalar", ve, chain_read[g])
                kb.op("scalar", "activation", tsf[:, :], rcs[:, :], AF.Sqrt,
                      bias=0.0, scale=1.0, inc=se)
                sqrt_done[g] = kb.val(se)
                kb.wait("vector", se, sqrt_done[g])
                kb.sop("vector", "tensor_copy", shi[:, par, :], tsf[:, :])
                kb.sop("vector", "tensor_tensor", slo[:, par, :], tsf[:, :],
                       shi[:, par, :], ALU.subtract)
                v_rows = kb.val(ve)

                if g >= 1:
                    kb.wait("tensor", ve, evac_done[(g - 1, "v")])
                for j in range(16):
                    for c in range(4):
                        kb.op("tensor", "matmul", ps_v[:, c, :],
                              xb16[:, par, j, 128 * c:128 * (c + 1)],
                              wv_bf[:, j, :], start=(j == 0), stop=False)
                xb_free[par] = kb.val(te) + 1

                if g >= 1:
                    emit_bound(g - 1)

                kb.wait("tensor", ve, v_rows)
                for pcp in range(4):
                    kb.op("tensor", "matmul", ps_q[pcp][:, :],
                          qsum[:, 128 * pcp:128 * (pcp + 1)], mrow[:, par, :],
                          start=False, stop=True,
                          inc=(te if pcp == 3 else None))
                kb.op("tensor", "matmul", ps_k[:, :], ksum[:, :],
                      mrow[:, par, :], start=False, stop=True)
                for c in range(4):
                    kb.op("tensor", "matmul", ps_v[:, c, :],
                          mrow[:, par, 128 * c:128 * (c + 1)], vsum[:, :],
                          start=False, stop=True)
                if "bc" in statq:
                    kb.wait("tensor", ve, statq["bc"])
                kb.op("tensor", "matmul", ps_bc[:, :], onesrow[:, :],
                      shi[:, par, :], start=True, stop=False)
                kb.op("tensor", "matmul", ps_bc[:, :], onesrow[:, :],
                      slo[:, par, :], start=False, stop=True, inc=te)
                v_corr = kb.val(te)
                corr_done[g] = v_corr
                vdone[g] = v_corr

                kb.wait("vector", stb[par], v_tab)
                kb.wait("vector", te, v_corr)
                kb.sop("vector", "tensor_tensor", tab_s[:, par, 0, :],
                       tabb[:, par, 0, :], ps_bc[:, :], ALU.mult)
                kb.sop("vector", "tensor_tensor", tab_s[:, par, 1, :],
                       tabb[:, par, 1, :], ps_bc[:, :], ALU.mult)
                tabfree[par] = kb.val(ve)

                for ip, proj in enumerate(PROJ):
                    pq = ps_q[ip] if proj != "k" else ps_k
                    sl = ip % 2
                    kb.sop("vector", "tensor_copy", Qc[:, sl, :], pq[:, :])
                    kb.wait("vector", ve)
                    for hh in range(2):
                        b0 = 64 * hh
                        kb.op("vector", "tensor_copy", rtm[b0:b0 + 32, sl, :],
                              Qc[b0 + 32:b0 + 64, sl, :])
                        kb.op("vector", "tensor_copy",
                              rtm[b0 + 32:b0 + 64, sl, :],
                              Qc[b0:b0 + 32, sl, :],
                              inc=(ve if hh == 1 else None))
                    evac_done[(g, proj)] = kb.val(ve)
                    kb.sop("vector", "tensor_tensor", rt1[:, sl, :],
                           Qc[:, sl, :], tab_s[:, par, 0, :], ALU.mult)
                    kb.sop("vector", "tensor_tensor", rt2[:, sl, :],
                           rtm[:, sl, :], tab_s[:, par, 1, :], ALU.mult)
                    kb.wait("vector", ve)
                    if proj == "k":
                        kb.op("vector", "tensor_tensor",
                              Kpa[0:64, pos0:pos0 + 512], rt1[0:64, sl, :],
                              rt2[0:64, sl, :], ALU.add)
                        kb.op("vector", "tensor_tensor",
                              Kpb[64:128, pos0:pos0 + 512],
                              rt1[64:128, sl, :], rt2[64:128, sl, :],
                              ALU.add, inc=ve)
                    else:
                        kb.op("vector", "tensor_tensor",
                              Q[:, ip, pos0:pos0 + 512], rt1[:, sl, :],
                              rt2[:, sl, :], ALU.add, inc=ve)
                ropev[g] = kb.val(ve)

                kb.wait("tensor", ve, tabfree[par])
                for c in range(4):
                    kb.op("tensor", "matmul", ps_bc[:, c:c + 1],
                          shi[:, par, 128 * c:128 * (c + 1)],
                          onescol[0:1, 0:1], start=True, stop=True,
                          inc=(te if c == 3 else None))
                trans_done[g] = kb.val(te)
                kb.wait("vector", te, trans_done[g])
                kb.op("vector", "tensor_copy", statc[:, 4 * g:4 * g + 4],
                      ps_bc[:, 0:4], inc=ve)
                statq["bc"] = kb.val(ve)
                kb.wait("vector", te, vdone[g])
                kb.wait("vector", ve)
                for c in range(4):
                    ci_ = 4 * g + c
                    kb.op("vector", "tensor_scalar", Va[:, ci_, 0:64],
                          ps_v[:, c, 0:64], statc[:, ci_:ci_ + 1], None,
                          ALU.mult)
                    kb.op("vector", "tensor_scalar", Vb[:, ci_, 0:64],
                          ps_v[:, c, 64:128], statc[:, ci_:ci_ + 1], None,
                          ALU.mult, inc=(ve if c == 3 else None))
                evac_done[(g, "v")] = kb.val(ve)

            emit_bound(NGQ - 1)

            kb.sop("vector", "tensor_tensor", k2run[:, 0:1], k2run[:, 0:1],
                   k2run[:, 1:2], ALU.max)
            kb.sop("vector", "tensor_scalar", s1t[:, 0:4], q2run[:, :],
                   k2run[:, 0:1], None, ALU.mult)
            kb.wait("scalar", ve)
            kb.op("scalar", "activation", s1t[:, 4:8], s1t[:, 0:4], AF.Sqrt,
                  bias=0.0, scale=1.0, inc=se)
            kb.wait("vector", se)
            kb.sop("vector", "tensor_scalar", s1t[:, 8:12], s1t[:, 4:8],
                   -SCALE, None, ALU.mult)
            kb.sop("vector", "tensor_copy", nbb[:, :], s1t[:, 8:12])
            kb.wait("tensor", ve)
            kb.op("tensor", "matmul", ps_bc[:, 4:8], onesrow[:, :], nbb[:, :],
                  start=True, stop=True, inc=te)
            kb.wait("vector", te)
            kb.op("vector", "tensor_copy", biasb[:, :], ps_bc[:, 4:8], inc=ve)
            v_bias = kb.val(ve)
            v_projdone = kb.val(te)
            v_P1se = kb.val(se)

            if debug:
                kb.wait("gpsimd", ve, v_bias)
                kb.dma("gpsimd", dbg["qsum"][:, :], qsum[:, :], sdb)
                kb.dma("gpsimd", dbg["mrow"][:, :, :], mrow[:, :, :], sdb)
                kb.dma("gpsimd", dbg["shi"][:, :, :], shi[:, :, :], sdb)
                kb.dma("gpsimd", dbg["statc"][:, :], statc[:, :], sdb)
                v_dbg1 = kb.val(sdb)
                kb.wait("tensor", sdb, v_dbg1)

        for e in ENGINES:
            kb.wait(e, te, v_projdone)
            kb.wait(e, ve, v_bias)
            kb.wait(e, se, v_P1se)

        with contextlib.ExitStack() as p2:
            A = p2.enter_context(sb("A", [128, 4, L], BF16))
            pb = p2.enter_context(sb("pb", [128, 6, 512], BF16))
            sst = p2.enter_context(sb("sst", [128, 8, 512], BF16))
            sigm = p2.enter_context(sb("sigm", [64, 512], BF16))
            rbs = p2.enter_context(sb("rbs", [64, 512], F32))
            recb = p2.enter_context(sb("recb", [64, 512], BF16))
            r0all = p2.enter_context(sb("r0all", [128, 16, 512], BF16))
            ones2d = p2.enter_context(sb("ones2d", [128, 64], BF16))

            ps_s = p2.enter_context(ps("ps_s", [128, 6, 512], F32))
            ps_pv = [p2.enter_context(ps(f"ps_pv{i}", [97, 512], F32)) for i in range(2)]

            kb.op("vector", "memset", ones2d[:, :], 1.0, inc=ve)
            sfree = {}
            pbfree = {}
            pvfree = None
            v_pvp = 0
            for pc in range(4):
                for qt in range(QTN):
                    q0 = 512 * qt
                    subs = [(kt, ks) for kt in range(qt + 1) for ks in range(4)]
                    nsub = len(subs)
                    stage = {}
                    for i in range(nsub + 1):
                        if i < nsub:
                            par = i % 3
                            kt, ks = subs[i]
                            k0 = 512 * kt + 128 * ks
                            for hh, Kp in enumerate((Kpa, Kpb)):
                                slot = 2 * par + hh
                                if slot in sfree:
                                    kb.wait("tensor", se, sfree[slot])
                                kb.op("tensor", "matmul", ps_s[:, slot, :],
                                      Kp[:, k0:k0 + 128], Q[:, pc, q0:q0 + 512],
                                      start=True, stop=True,
                                      inc=(te if hh == 1 else None))
                            stage[i] = kb.val(te)
                            kt, ks = subs[i]
                            diag = (kt == qt)
                            kb.wait("scalar", te, stage[i])
                            for hh in range(2):
                                slot = 2 * par + hh
                                if slot in pbfree:
                                    kb.wait("scalar", te, pbfree[slot])
                            cs = 128 * ks if diag else 0
                            kb.op("scalar", "activation",
                                  pb[:, 2 * par:2 * par + 2, cs:512],
                                  ps_s[:, 2 * par:2 * par + 2, cs:512],
                                  AF.Exp, bias=biasb[:, pc:pc + 1],
                                  scale=SCALE, inc=se)
                            v_exp = kb.val(se)
                            sfree[2 * par] = v_exp
                            sfree[2 * par + 1] = v_exp
                            v_mask = None
                            if diag:
                                kb.wait("vector", se, v_exp)
                                if ks > 0:
                                    kb.op("vector", "memset",
                                          pb[:, 2 * par:2 * par + 2,
                                             0:128 * ks], 0.0)
                                for hh in range(2):
                                    slot = 2 * par + hh
                                    kb.op("vector", "tensor_tensor",
                                          pb[:, slot, 128 * ks:128 * (ks + 1)],
                                          pb[:, slot, 128 * ks:128 * (ks + 1)],
                                          tri[:, :], ALU.mult,
                                          inc=(ve if hh == 1 else None))
                                v_mask = kb.val(ve)
                            stage[i] = (v_exp, v_mask)
                        if i >= 1:
                            j = i - 1
                            par = j % 3
                            kt, ks = subs[j]
                            ci_ = 4 * kt + ks
                            v_exp, v_mask = stage[j]
                            if j == 0 and pvfree is not None:
                                kb.wait("tensor", ve, pvfree)
                            if v_mask is not None:
                                kb.wait("tensor", ve, v_mask)
                            else:
                                kb.wait("tensor", se, v_exp)
                            for hh, Vx in enumerate((Va, Vb)):
                                slot = 2 * par + hh
                                kb.op("tensor", "matmul", ps_pv[hh][:, :],
                                      Vx[:, ci_, 0:97], pb[:, slot, :],
                                      start=(j == 0), stop=(j == nsub - 1),
                                      inc=(te if hh == 1 else None))
                            v_pvp = kb.val(te)
                            pbfree[2 * par] = v_pvp
                            pbfree[2 * par + 1] = v_pvp
                    kb.wait("vector", te, v_pvp)
                    for hh in range(2):
                        r = (pc * 2 + hh) * 8 + qt
                        slot = r % 8
                        kb.op("vector", "tensor_copy",
                              A[64 * hh:64 * (hh + 1), pc, q0:q0 + 512],
                              ps_pv[hh][0:64, :])
                        if r >= 8:
                            kb.wait("vector", ssst[slot])
                        kb.op("vector", "tensor_copy", sst[96:97, slot, :],
                              ps_pv[hh][96:97, :], inc=ve)
                        v_sst = kb.val(ve)
                        kb.wait("gpsimd", ve, v_sst)
                        kb.dma("gpsimd", sigm[r:r + 1, :], sst[96:97, slot, :],
                               ssst[slot])
                    pvfree = kb.val(ve)

            for s8 in range(8):
                kb.wait("vector", ssst[s8])
            kb.sop("vector", "reciprocal", rbs[:, :], sigm[:, :])
            kb.sop("vector", "tensor_copy", recb[:, :], rbs[:, :])
            v_rec = kb.val(ve)
            kb.wait("gpsimd", ve, v_rec)
            for g4 in range(4):
                kb.dma("gpsimd", r0all[32 * g4:32 * g4 + 1, :, :],
                       recb[g4:64:4, :], srb[0])
            v_rb = kb.val(srb[0])
            bcfree = {}
            kb.wait("tensor", srb[0], v_rb)
            for r in range(64):
                pcr, hh, qtr = r // 16, (r % 16) // 8, r % 8
                par = r % 4
                pgr = 32 * (r % 4)
                if par in bcfree:
                    kb.wait("tensor", ve, bcfree[par])
                kb.op("tensor", "matmul", ps_s[0:64, par, :],
                      ones2d[pgr:pgr + 1, :], r0all[pgr:pgr + 1, r // 4, :],
                      start=True, stop=True, tile_position=(pgr, 0), inc=te)
                kb.wait("vector", te)
                kb.op("vector", "tensor_tensor",
                      A[64 * hh:64 * (hh + 1), pcr, 512 * qtr:512 * (qtr + 1)],
                      A[64 * hh:64 * (hh + 1), pcr, 512 * qtr:512 * (qtr + 1)],
                      ps_s[0:64, par, :], ALU.mult, inc=ve)
                bcfree[par] = kb.val(ve)
            v_Adone = kb.val(ve)

            if debug:
                kb.wait("gpsimd", ve, v_Adone)
                kb.dma("gpsimd", dbg["sigm"][:, :], sigm[:, :], sdb)
                kb.dma("gpsimd", dbg["biasb"][:, :], biasb[:, :], sdb)
                kb.dma("gpsimd", dbg["A"][:, :, :], A[:, :, :], sdb)
                kb.dma("gpsimd", dbg["Q"][:, :, :], Q[:, :, :], sdb)
                kb.dma("gpsimd", dbg["Kpa"][:, :], Kpa[:, :], sdb)
                kb.dma("gpsimd", dbg["Kpb"][:, :], Kpb[:, :], sdb)
                kb.dma("gpsimd", dbg["Va"][:, :, :], Va[:, :, :], sdb)

            woT_bf = p2.enter_context(sb("woT_bf", [128, 4, D], BF16))
            wst2 = p2.enter_context(sb("wst2", [128, 2, D], F32))
            ccst = p2.enter_context(sb("ccst", [128, 2, D], BF16))

            wofree = {}
            if debug:
                kb.wait("sync", sdb)
            for ch in range(4):
                par = ch % 2
                if par in wofree:
                    kb.wait("sync", ve, wofree[par])
                kb.dma("sync", wst2[:, par, :], woT[128 * ch:128 * (ch + 1), :],
                       swt[par])
                kb.wait("vector", swt[par])
                kb.op("vector", "tensor_copy", woT_bf[:, ch, :],
                      wst2[:, par, :], inc=ve)
                wofree[par] = kb.val(ve)
            v_wo = kb.val(ve)
            kb.wait("tensor", ve, max(v_wo, v_Adone))
            ofree = {}
            ccfree = {}
            for t in range(NT):
                tp = t % 2
                if tp in ccfree:
                    kb.wait("vector", scm[tp], ccfree[tp])
                for dq in range(4):
                    bank = dq % 2
                    if bank in ofree:
                        kb.wait("tensor", ve, ofree[bank])
                    for pc in range(4):
                        kb.op("tensor", "matmul", ps_s[:, bank, :],
                              A[:, pc, 128 * t:128 * (t + 1)],
                              woT_bf[:, pc, 512 * dq:512 * (dq + 1)],
                              start=(pc == 0), stop=(pc == 3),
                              inc=(te if pc == 3 else None))
                    kb.wait("vector", te)
                    kb.op("vector", "tensor_copy",
                          ccst[:, tp, 512 * dq:512 * (dq + 1)],
                          ps_s[:, bank, :], inc=ve)
                    ofree[bank] = kb.val(ve)
                v_cc = kb.val(ve)
                kb.wait("gpsimd", ve, v_cc)
                ccfree[tp] = kb.dma("gpsimd",
                                    cc_in.ap()[128 * t:128 * (t + 1), :],
                                    ccst[:, tp, :], scm[tp])
            v_woD = kb.val(te)

        kb.wait("gpsimd", scm[0])
        kb.wait("gpsimd", scm[1])
        kb.op("gpsimd", "collective_compute", "ReduceScatter", ALU.add,
              replica_groups=[[0, 1, 2, 3], [4, 5, 6, 7]],
              ins=[cc_in.ap().opt()], outs=[cc_out.ap().opt()], inc=scc)

        es_attn.close()
        with contextlib.ExitStack() as p3:
            acc = p3.enter_context(sb("acc", [128, 8, D], F32))
            xqT = p3.enter_context(sb("xqT", [128, 16, OWN], BF16))
            xnf = p3.enter_context(sb("xnf", [128, D], F32))
            r16t = p3.enter_context(sb("r16t", [128, D], FP16))
            wgf = p3.enter_context(sb("wgf", [128, 2, 16, 128], F32))
            wuf = p3.enter_context(sb("wuf", [128, 2, 16, 128], F32))
            wdf = p3.enter_context(sb("wdf", [128, 2, 2048], F32))
            wg_s = p3.enter_context(sb("wg_s", [128, 3, 16, 128], FP8))
            wu_s = p3.enter_context(sb("wu_s", [128, 3, 16, 128], FP8))
            wd_s = p3.enter_context(sb("wd_s", [128, 8, D], FP8))
            p_ring = p3.enter_context(sb("p_ring", [128, 8, OWN], BF16))
            sil = p3.enter_context(sb("sil", [128, 2, 512], BF16))
            rsb = p3.enter_context(sb("rsb", [128, D], BF16))
            jnk3 = rsb
            xqb = rsb

            ps_g = [p3.enter_context(ps(f"ps_g{i}", [128, 512], F32)) for i in range(2)]
            ps_u = [p3.enter_context(ps(f"ps_u{i}", [128, 512], F32)) for i in range(2)]
            ps_o = [p3.enter_context(ps(f"ps_o{i}", [128, 512], F32)) for i in range(2)]
            ps_t3 = p3.enter_context(ps("ps_t3", [128, 512], BF16))

            kb.wait("sync", te, v_woD)
            kb.wait("sync", scm[0])
            kb.wait("sync", scm[1])
            if debug:
                kb.wait("sync", sdb)
            for t in range(8):
                kb.dma("sync", acc[:, t, :], xo[128 * t:128 * (t + 1), :], sac)
            v_acc0 = kb.val(sac)

            t3free = None
            t3free_v = None
            for t in range(8):
                at = acc[:, t, :]
                kb.wait("vector", sac, v_acc0)
                kb.sop("vector", "tensor_reduce", smt[:, 0:1], at, AX.X,
                       ALU.add)
                kb.wait("scalar", sac, v_acc0)
                if t3free is not None:
                    kb.wait("scalar", te, t3free)
                kb.op("scalar", "activation", jnk3[:, :], at, AF.Square,
                      bias=0.0, scale=1.0, accum_out=smt[:, 15:16], inc=se)
                kb.sop("vector", "tensor_scalar", smt[:, 2:3], smt[:, 0:1],
                       1.0 / D, None, ALU.mult)
                kb.wait("vector", se)
                kb.sop("vector", "tensor_scalar", smt[:, 3:4], smt[:, 15:16],
                       1.0 / D, None, ALU.mult)
                kb.sop("vector", "tensor_tensor", smt[:, 4:5], smt[:, 2:3],
                       smt[:, 2:3], ALU.mult)
                kb.sop("vector", "tensor_tensor", smt[:, 5:6], smt[:, 3:4],
                       smt[:, 4:5], ALU.subtract)
                kb.sop("vector", "tensor_scalar", smt[:, 6:7], smt[:, 5:6],
                       EPS_LN, None, ALU.add)
                kb.sop("vector", "reciprocal", smt[:, 7:8], smt[:, 6:7])
                kb.wait("scalar", ve)
                kb.op("scalar", "activation", smt[:, 8:9], smt[:, 7:8], AF.Sqrt,
                      bias=0.0, scale=1.0, inc=se)
                kb.wait("vector", se)
                kb.sop("vector", "tensor_scalar", xnf[:, :], at,
                       smt[:, 2:3], smt[:, 8:9], ALU.subtract, op1=ALU.mult)
                kb.sop("vector", "tensor_reduce", smt[:, 9:10], xnf[:, :],
                       AX.X, ALU.add)
                kb.wait("scalar", ve)
                kb.op("scalar", "activation", jnk3[:, :], xnf[:, :], AF.Square,
                      bias=0.0, scale=1.0, accum_out=smt[:, 10:11], inc=se)
                kb.sop("vector", "tensor_scalar", smt[:, 11:12], smt[:, 9:10],
                       1.0 / D, None, ALU.mult)
                kb.wait("vector", se)
                kb.sop("vector", "tensor_scalar", smt[:, 12:13], smt[:, 10:11],
                       1.0 / D, None, ALU.mult)
                kb.sop("vector", "tensor_tensor", smt[:, 13:14], smt[:, 11:12],
                       smt[:, 11:12], ALU.mult)
                kb.sop("vector", "tensor_tensor", smt[:, 12:13], smt[:, 12:13],
                       smt[:, 13:14], ALU.subtract)
                kb.sop("vector", "tensor_scalar", smt[:, 12:13], smt[:, 12:13],
                       EPS_RMS, None, ALU.add)
                kb.sop("vector", "reciprocal", smt[:, 13:14], smt[:, 12:13])
                kb.wait("scalar", ve)
                kb.op("scalar", "activation", smt[:, 14:15], smt[:, 13:14],
                      AF.Sqrt, bias=0.0, scale=1.0, inc=se)
                kb.sop("vector", "tensor_scalar", xnf[:, :], xnf[:, :],
                       smt[:, 11:12], None, ALU.subtract)
                kb.sop("vector", "tensor_reduce", smt[:, 0:1], xnf[:, :],
                       AX.X, ALU.max, apply_absolute_value=True)
                kb.sop("vector", "reciprocal", smt[:, 1:2], smt[:, 0:1])
                kb.sop("vector", "tensor_scalar", smt[:, 2:3], smt[:, 1:2],
                       127.0, None, ALU.mult)
                kb.sop("vector", "tensor_scalar", r16t[:, :], xnf[:, :],
                       smt[:, 2:3], MAGIC, ALU.mult, op1=ALU.add)
                kb.wait("vector", se)
                kb.sop("vector", "tensor_tensor", smt[:, 3:4], smt[:, 0:1],
                       smt[:, 14:15], ALU.mult)
                kb.sop("vector", "tensor_scalar", smt[:, 4:5], smt[:, 3:4],
                       1.0 / 127.0, None, ALU.mult)
                if t3free is not None:
                    kb.wait("vector", te, t3free)
                kb.sop("vector", "tensor_scalar", xqb[:, :], r16t[:, :],
                       MAGIC, smt[:, 4:5], ALU.subtract, op1=ALU.mult)
                v_xq = kb.val(ve)
                kb.wait("tensor", ve, v_xq)
                for qd in range(4):
                    if t3free_v is not None:
                        kb.wait("tensor", ve, t3free_v)
                    for jj in range(4):
                        j = 4 * qd + jj
                        kb.op("tensor", "transpose",
                              ps_t3[:, 128 * jj:128 * (jj + 1)],
                              xqb[:, 128 * j:128 * (j + 1)], idn[:, :],
                              inc=(te if jj == 3 else None))
                    kb.wait("vector", te)
                    kb.op("vector", "tensor_copy",
                          xqT[:, 4 * qd:4 * (qd + 1), 128 * t:128 * (t + 1)],
                          ps_t3[:, :].rearrange("p (j c) -> p j c", j=4),
                          inc=ve)
                    t3free_v = kb.val(ve)
                    t3free = kb.val(te)
            v_xqT = kb.val(ve)
            if debug:
                kb.wait("gpsimd", ve, v_xqT)
                kb.dma("gpsimd", dbg["xqT"][:, :, :], xqT[:, :, :], sdb)

            kb.wait("tensor", ve, v_xqT)
            kb.wait("vector", te, t3free)
            kb.wait("gpsimd", te, v_woD)
            if debug:
                kb.wait("gpsimd", sdb)
            sgn_gu = {}
            wdsv = {}
            wddma = {}
            gu_done = {}
            wd_done = {}
            gfree = {}
            ufree = {}
            o_free = {}
            silfree = {}

            def emit_wdma(hc):
                par = hc % 2
                if hc >= 2:
                    kb.wait("gpsimd", se, sgn_gu[hc - 2])
                    kb.wait("sync", se, sgn_gu[hc - 2])
                kb.dma("gpsimd", wgf[:, par, :, :],
                       wgT[hc, :, :].rearrange("p (j h) -> p j h", j=16),
                       swg[par])
                kb.dma("sync", wuf[:, par, :, :],
                       wuT[hc, :, :].rearrange("p (j h) -> p j h", j=16),
                       swu[par])
                if hc >= 2:
                    kb.wait("sync", ve, wdsv[hc - 2])
                kb.dma("sync", wdf[:, par, :],
                       wdT[128 * hc:128 * (hc + 1), :], swd[par])
                wddma[hc] = kb.val(swd[par])

            def emit_signs(hc):
                par = hc % 2
                s4 = hc % 3
                kb.wait("scalar", swg[par], kb.val(swg[par]))
                if hc >= 3:
                    kb.wait("scalar", te, gu_done[hc - 3])
                kb.op("scalar", "activation", wg_s[:, s4, :, :],
                      wgf[:, par, :, :], AF.Sign, bias=0.0, scale=1.0)
                kb.wait("scalar", swu[par], kb.val(swu[par]))
                kb.op("scalar", "activation", wu_s[:, s4, :, :],
                      wuf[:, par, :, :], AF.Sign, bias=0.0, scale=1.0, inc=se)
                sgn_gu[hc] = kb.val(se)

            def emit_compute(hc):
                par = hc % 2
                slot8 = hc % 8
                s4 = hc % 3
                kb.wait("tensor", se, sgn_gu[hc])
                for th in range(2):
                    gb = th
                    if gb in gfree:
                        kb.wait("tensor", se, gfree[gb])
                        kb.wait("tensor", ve, ufree[gb])
                    for j in range(16):
                        kb.op("tensor", "matmul", ps_g[gb][:, :],
                              wg_s[:, s4, j, :],
                              xqT[:, j, 512 * th:512 * (th + 1)],
                              start=(j == 0), stop=(j == 15),
                              inc=(te if j == 15 else None))
                    for j in range(16):
                        kb.op("tensor", "matmul", ps_u[gb][:, :],
                              wu_s[:, s4, j, :],
                              xqT[:, j, 512 * th:512 * (th + 1)],
                              start=(j == 0), stop=(j == 15),
                              inc=(te if j == 15 else None))
                    v_gu = kb.val(te)
                    kb.wait("scalar", te, v_gu)
                    if gb in silfree:
                        kb.wait("scalar", ve, silfree[gb])
                    kb.op("scalar", "activation", sil[:, gb, :],
                          ps_g[gb][:, :], AF.Silu, bias=0.0, scale=1.0, inc=se)
                    v_sil = kb.val(se)
                    gfree[gb] = v_sil
                    kb.wait("vector", se, v_sil)
                    if hc >= 8 and th == 0:
                        kb.wait("vector", te, wd_done[hc // 8 - 1])
                    kb.op("vector", "tensor_tensor",
                          p_ring[:, slot8, 512 * th:512 * (th + 1)],
                          sil[:, gb, :], ps_u[gb][:, :], ALU.mult, inc=ve)
                    silfree[gb] = kb.val(ve)
                    ufree[gb] = kb.val(ve)
                gu_done[hc] = kb.val(te)
                kb.wait("vector", swd[par], wddma[hc])
                if hc >= 8:
                    kb.wait("vector", te, wd_done[hc // 8 - 1])
                kb.sop("vector", "tensor_scalar", jnk3[:, :], wdf[:, par, :],
                       0.0, None, ALU.is_gt)
                kb.sop("vector", "tensor_scalar", r16t[:, :], wdf[:, par, :],
                       0.0, None, ALU.is_lt)
                kb.sop("vector", "tensor_tensor", wd_s[:, slot8, :],
                       jnk3[:, :], r16t[:, :], ALU.subtract)
                wdsv[hc] = kb.val(ve)
                if slot8 == 7:
                    kb.wait("tensor", ve, wdsv[hc])
                    for t8 in range(8):
                        for dq in range(4):
                            ob = (t8 * 4 + dq) % 2
                            if ob in o_free:
                                kb.wait("tensor", ve, o_free[ob])
                            for h8 in range(8):
                                kb.op("tensor", "matmul", ps_o[ob][:, :],
                                      p_ring[:, h8, 128 * t8:128 * (t8 + 1)],
                                      wd_s[:, h8, 512 * dq:512 * (dq + 1)],
                                      start=(h8 == 0), stop=(h8 == 7),
                                      inc=(te if h8 == 7 else None))
                            v_od = kb.val(te)
                            kb.wait("vector", te, v_od)
                            kb.op("vector", "tensor_tensor",
                                  acc[:, t8, 512 * dq:512 * (dq + 1)],
                                  acc[:, t8, 512 * dq:512 * (dq + 1)],
                                  ps_o[ob][:, :], ALU.add, inc=ve)
                            o_free[ob] = kb.val(ve)
                    wd_done[hc // 8] = kb.val(te)

            for it in range(66):
                if it >= 2:
                    emit_compute(it - 2)
                if it < 64:
                    emit_wdma(it)
                    emit_signs(it)

            accflush = None
            kb.wait("gpsimd", scc, 1)
            kb.wait("gpsimd", ve, wdsv[63])
            for t in range(8):
                if accflush is not None:
                    kb.wait("gpsimd", ve, accflush)
                v_rs = kb.dma("gpsimd", rsb[:, :],
                              cc_out.ap()[128 * t:128 * (t + 1), :], scm[0])
                kb.wait("vector", scm[0], v_rs)
                kb.op("vector", "tensor_tensor", acc[:, t, :], acc[:, t, :],
                      rsb[:, :], ALU.add, inc=ve)
                accflush = kb.val(ve)
            v_out = kb.val(ve)
            kb.wait("gpsimd", ve, v_out)
            for t in range(8):
                kb.dma("gpsimd", y[128 * t:128 * (t + 1), :], acc[:, t, :], syy)
            kb.wait("gpsimd", syy)

        with nc.Block() as block:
            @block.sync
            def _(e):
                for f in kb.streams["sync"]:
                    f(e)

            @block.tensor
            def _(e):
                for f in kb.streams["tensor"]:
                    f(e)

            @block.vector
            def _(e):
                for f in kb.streams["vector"]:
                    f(e)

            @block.scalar
            def _(e):
                for f in kb.streams["scalar"]:
                    f(e)

            @block.gpsimd
            def _(e):
                for f in kb.streams["gpsimd"]:
                    f(e)
    return nc


_SPLIT = np.concatenate([np.arange(0, 64, 2), np.arange(1, 64, 2)])


def _prep(inputs):
    x = np.ascontiguousarray(np.asarray(inputs["L0_attn_x"], dtype=np.float32))
    Wq = np.asarray(inputs["L0_attn_Wq"], dtype=np.float32)
    Wk = np.asarray(inputs["L0_attn_Wk"], dtype=np.float32)
    Wv = np.asarray(inputs["L0_attn_Wv"], dtype=np.float32)
    Wo = np.asarray(inputs["L0_attn_Wo"], dtype=np.float32)
    Wg = np.asarray(inputs["L0_ffn_W_g"], dtype=np.float32)
    Wu = np.asarray(inputs["L0_ffn_W_u"], dtype=np.float32)
    Wd = np.asarray(inputs["L0_ffn_W_d"], dtype=np.float32)

    wgT = np.ascontiguousarray(
        Wg.T.reshape(16, 128, 64, 128).transpose(2, 1, 0, 3).reshape(64, 128, 2048))
    wuT = np.ascontiguousarray(
        Wu.T.reshape(16, 128, 64, 128).transpose(2, 1, 0, 3).reshape(64, 128, 2048))
    wdT = np.ascontiguousarray(Wd.T)

    inv = 10000.0 ** (-np.arange(0, HD, 2, dtype=np.float64) / HD)
    pos = np.arange(L, dtype=np.float64)
    f = inv[:, None] * pos[None, :]
    c = np.cos(f).astype(np.float32)
    s = np.sin(f).astype(np.float32)
    CC = np.tile(c, (4, 1))
    SS = np.concatenate([-s, s, -s, s], axis=0)
    cs = np.ascontiguousarray(np.stack([CC, SS]))

    tri = np.triu(np.ones((128, 128), dtype=np.float32)).astype(bfdt)
    idn = np.eye(128, dtype=bfdt)
    onc = np.ones((128, 1), dtype=bfdt)
    onr = np.ones((1, 128), dtype=bfdt)

    xT = [np.ascontiguousarray(
        x[b].T.reshape(4, 4, 128, 8, 512).transpose(3, 0, 2, 1, 4)
        .reshape(32, 128, 2048)) for b in range(B)]

    maps = []
    for c_ in range(8):
        b, j = c_ // 4, c_ % 4
        qheads = [8 * j + 4 * hh + pc for pc in range(4) for hh in range(2)]
        qrows = np.concatenate([64 * h + _SPLIT for h in qheads])
        wqT_c = np.ascontiguousarray(
            Wq[qrows].T.reshape(16, 128, 4, 128).transpose(2, 1, 0, 3)
            .reshape(4, 128, 2048))
        krows = np.concatenate([64 * (2 * j) + _SPLIT, 64 * (2 * j + 1) + _SPLIT])
        wkT_c = np.ascontiguousarray(
            Wk[krows].T.reshape(16, 128, 128).transpose(1, 0, 2)
            .reshape(1, 128, 2048))
        vrows = np.arange(64 * 2 * j, 64 * (2 * j + 2))
        wvT_c = np.ascontiguousarray(
            Wv[vrows].T.reshape(16, 128, 128).transpose(1, 0, 2)
            .reshape(1, 128, 2048))
        orows = np.concatenate([np.arange(64 * h, 64 * (h + 1)) for h in qheads])
        woT_c = np.ascontiguousarray(Wo.T[orows])
        maps.append({
            "xfT": xT[b],
            "xo": np.ascontiguousarray(x[b, 1024 * j:1024 * (j + 1)]),
            "wqT": wqT_c, "wkT": wkT_c, "wvT": wvT_c, "woT": woT_c,
            "wgT": wgT, "wuT": wuT, "wdT": wdT,
            "cs_tab": cs, "tri_in": tri, "idn_in": idn,
            "onc_in": onc, "onr_in": onr,
        })
    return maps


_NC_CACHE = {}


def kernel(**inputs):
    if "nc" not in _NC_CACHE:
        _NC_CACHE["nc"] = build_nc(debug=False)
    nc = _NC_CACHE["nc"]
    maps = _prep(inputs)
    res = run_bass_kernel_spmd(nc, maps, core_ids=list(range(8)))
    out = np.empty((B, L, D), dtype=np.float32)
    for c_ in range(8):
        b, j = c_ // 4, c_ % 4
        out[b, 1024 * j:1024 * (j + 1)] = res.results[c_]["y"]
    return out
